# version 05
# baseline (speedup 1.0000x reference)
import contextlib
import numpy as np

import concourse.bass as bass
import concourse.mybir as mybir
from concourse.bass_utils import run_bass_kernel_spmd

F32 = mybir.dt.float32
BF16 = mybir.dt.bfloat16
AL = mybir.AluOpType
AF = mybir.ActivationFunctionType
AX = mybir.AxisListType

NB = 2
T = 2048
DM = 1024
DI = 512
NS = 256
NEXP = 8
FF = 4096
EPS = 1e-5
DMC = DM // 128
DIC = DI // 128
NSC = NS // 128

ENGS = ("sync", "vector", "scalar", "tensor", "gpsimd")
PHASE_LIMIT = None
MILESTONES = []


class Sem:
    __slots__ = ("h", "n")

    def __init__(self, nc, es, name):
        self.h = es.enter_context(nc.semaphore(name))
        self.n = 0


class Phase:
    def __init__(self, nc, bar, idx):
        self.bar = bar
        self.idx = idx
        self.blk = nc.Block()
        self.done = set()

    def __enter__(self):
        self.blk.__enter__()
        return self

    def sec(self, eng_name, fn):
        assert eng_name not in self.done, eng_name
        self.done.add(eng_name)
        if PHASE_LIMIT is not None and self.idx >= PHASE_LIMIT:
            return
        target = 5 * self.idx
        dec = getattr(self.blk, eng_name)

        def wrapped(eng):
            eng.wait_ge(self.bar.h, target)
            fn(eng)
            eng.maybe_drain_then_inc((self.bar.h, 1))

        dec(wrapped)
        self.bar.n += 1

    def __exit__(self, *a):
        for e in ENGS:
            if e not in self.done:
                self.sec(e, lambda _eng: None)
        return self.blk.__exit__(*a)


class View:
    __slots__ = ("arena", "off", "n", "p")

    def __init__(self, arena, off, n, p=128):
        self.arena = arena
        self.off = off
        self.n = n
        self.p = p

    def __getitem__(self, key):
        ps, fs = key
        start = fs.start if fs.start is not None else 0
        stop = fs.stop if fs.stop is not None else self.n
        assert 0 <= start <= stop <= self.n, (start, stop, self.n)
        return self.arena.t[ps, self.off + start: self.off + stop]

    def mkap(self, extra_off, dims):
        base = self.arena.t.ap()
        return bass.AP(base.tensor, base.offset + self.off + extra_off, dims)


class Arena:
    def __init__(self, nc, es, name, dtype, nelem):
        self.t = es.enter_context(nc.sbuf_tensor(name, [128, nelem], dtype))
        self.cap = nelem
        self.free = [(0, nelem)]
        self.name = name

    def alloc(self, nelem, p=128):
        nelem_a = (nelem + 127) & ~127
        for i, (off, sz) in enumerate(self.free):
            if sz >= nelem_a:
                if sz == nelem_a:
                    self.free.pop(i)
                else:
                    self.free[i] = (off + nelem_a, sz - nelem_a)
                return View(self, off, nelem, p)
        raise MemoryError(f"arena {self.name}: need {nelem_a}, free={self.free}")

    def release(self, view):
        nelem_a = (view.n + 127) & ~127
        self.free.append((view.off, nelem_a))
        self.free.sort()
        merged = []
        for off, sz in self.free:
            if merged and merged[-1][0] + merged[-1][1] == off:
                merged[-1] = (merged[-1][0], merged[-1][1] + sz)
            else:
                merged.append((off, sz))
        self.free = [tuple(x) for x in merged]


def build():
    nc = bass.Bass("TRN2", target_bir_lowering=False)

    x_in = nc.dram_tensor("x_in", [NB, T, DM], F32, kind="ExternalInput").ap()
    gi = {}
    for name, shape in [
        ("ln_gamma", [DM]), ("ln_beta", [DM]), ("W_up", [DM, DM]),
        ("b_up", [DM]), ("conv_w", [DI, 1, 3]), ("conv_b", [DI]),
        ("W_delta", [DI, DI]), ("b_delta", [DI]), ("W_B", [DI, NS]),
        ("W_C", [DI, NS]), ("A_log", [DI, NS]), ("Dvec", [DI]),
        ("W_down", [DI, DM]), ("b_down", [DM]), ("gate_w", [NEXP, DM]),
        ("w1", [NEXP, DM, FF]), ("w2", [NEXP, FF, DM]),
    ]:
        gi[name] = nc.dram_tensor(name, shape, F32, kind="ExternalInput").ap()
    y_out = nc.dram_tensor("y_out", [NB, T, DM], F32, kind="ExternalOutput").ap()

    es = contextlib.ExitStack()
    with es:
        def sb(name, shape, dt, stack=None):
            return (stack or es).enter_context(nc.sbuf_tensor(name, shape, dt))

        def psum(name, shape, dt, stack=None):
            return (stack or es).enter_context(nc.psum_tensor(name, shape, dt))

        ident = sb("ident", [128, 128], F32)
        ident_bf = sb("ident_bf", [128, 128], BF16)
        ones_f = sb("ones_f", [128, 128], F32)
        ones_bf = sb("ones_bf", [128, 1], BF16)
        cw_cols = sb("cw_cols", [128, DIC * 3], F32)
        bup_c = sb("bup_c", [128, DMC], F32)
        convb_c = sb("convb_c", [128, DIC], F32)
        bdelta_c = sb("bdelta_c", [128, DIC], F32)
        dvec_c = sb("dvec_c", [128, DIC], F32)
        bdown_c = sb("bdown_c", [128, DMC], F32)
        AT = sb("AT", [128, NSC * DI], F32)
        gwT = sb("gwT", [128, DMC * 8], F32)
        ngs_row = sb("ngs_row", [1, 8], F32)
        statc = sb("statc", [128, 16], F32)
        rmax = sb("rmax", [128, 1], F32)

        af = Arena(nc, es, "af", F32, 22528)
        ab = Arena(nc, es, "ab", BF16, 57344)

        bar = Sem(nc, es, "bar")
        s1 = Sem(nc, es, "s1")
        s2 = Sem(nc, es, "s2")
        s3 = Sem(nc, es, "s3")
        s4 = Sem(nc, es, "s4")
        s5 = Sem(nc, es, "s5")
        s6 = Sem(nc, es, "s6")
        s7 = Sem(nc, es, "s7")
        s8 = Sem(nc, es, "s8")
        sOut = Sem(nc, es, "sOut")
        sVd = Sem(nc, es, "sVd")
        sAd = Sem(nc, es, "sAd")

        def vdep(v, ins):
            sVd.n += 1
            ins.then_inc(sVd.h, 1)
            v.wait_ge(sVd.h, sVd.n)

        def adep(s, ins):
            sAd.n += 1
            ins.then_inc(sAd.h, 1)
            s.wait_ge(sAd.h, sAd.n)

        pidx = [0]

        def phase():
            p = Phase(nc, bar, pidx[0])
            pidx[0] += 1
            return p

        ini = contextlib.ExitStack()
        iot = sb("iot", [128, 128], F32, ini)
        rowbuf = af.alloc(1024, p=1)
        gw_rows = af.alloc(DM, p=8)
        A_t = af.alloc(DIC * NS)

        with phase() as p:
            def f_gp(gp):
                gp.iota(iot[:, 0:128], [[1, 128]], channel_multiplier=-1,
                        allow_small_or_imprecise_dtypes=True).then_inc(s2.h, 1)
                s2.n += 1
                gp.dma_start(out=A_t[:, :],
                             in_=gi["A_log"].rearrange("(c p) n -> p c n", p=128)
                             ).then_inc(s2.h, 16)
                s2.n += 16
                gp.dma_start(out=cw_cols[:, :],
                             in_=gi["conv_w"].rearrange("(c p) u k -> p c u k",
                                                        p=128)
                             ).then_inc(s2.h, 16)
                s2.n += 16
            p.sec("gpsimd", f_gp)

            def f_v(v):
                v.wait_ge(s2.h, 1)
                vdep(v, v.tensor_single_scalar(ident[:, :], iot[:, 0:128],
                                               0.0, AL.is_equal))
                v.tensor_copy(ident_bf[:, :], ident[:, :])
                v.memset(ones_f[:, :], 1.0)
                v.memset(ones_bf[:, :], 1.0)
            p.sec("vector", f_v)

        vec_loads = [
            ("b_up", bup_c, DMC), ("conv_b", convb_c, DIC),
            ("b_delta", bdelta_c, DIC), ("Dvec", dvec_c, DIC),
            ("b_down", bdown_c, DMC),
        ]
        ps_col = psum("ps_col", [128, 16], F32, ini)
        for (srcname, dst, nch) in vec_loads:
            src = gi[srcname]
            with phase() as p:
                def f_sy(sy, src=src):
                    sy.dma_start(out=rowbuf[0:1, 0:src.shape[0]],
                                 in_=src.rearrange("(a d) -> a d", a=1)
                                 ).then_inc(s1.h, 16)
                    sy.wait_ge(s1.h, s1.n + 16)
                p.sec("sync", f_sy)
                s1.n += 16
            with phase() as p:
                def f_t(t, nch=nch):
                    for c in range(nch):
                        t.transpose(ps_col[:, c:c + 1],
                                    rowbuf[0:1, c * 128:(c + 1) * 128],
                                    ident[0:1, 0:1])
                p.sec("tensor", f_t)
            with phase() as p:
                p.sec("vector", lambda v, dst=dst, nch=nch:
                      v.tensor_copy(dst[:, 0:nch], ps_col[:, 0:nch]))

        ps_ngs = psum("ps_ngs", [1, 8], F32, ini)
        with phase() as p:
            def f_sy(sy):
                sy.dma_start(out=gw_rows[0:8, :], in_=gi["gate_w"][:, :]
                             ).then_inc(s1.h, 16)
                sy.wait_ge(s1.h, s1.n + 16)
            p.sec("sync", f_sy)
            s1.n += 16
        for dc in range(DMC):
            with phase() as p:
                p.sec("tensor", lambda t, dc=dc:
                      t.transpose(ps_col[:, 0:8],
                                  gw_rows[0:8, dc * 128:(dc + 1) * 128],
                                  ident[0:8, 0:8]))
            with phase() as p:
                p.sec("vector", lambda v, dc=dc:
                      v.tensor_copy(gwT[:, dc * 8:(dc + 1) * 8], ps_col[:, 0:8]))
        with phase() as p:
            def f_t(t):
                for dc in range(DMC):
                    t.matmul(ps_ngs[0:1, 0:8], ones_f[:, 0:1],
                             gwT[:, dc * 8:(dc + 1) * 8],
                             start=(dc == 0), stop=(dc == DMC - 1))
            p.sec("tensor", f_t)
        with phase() as p:
            p.sec("vector", lambda v:
                  v.tensor_scalar_mul(ngs_row[0:1, :], ps_ngs[0:1, :], -1.0))

        with phase() as p:
            def f_s(s):
                s.wait_ge(s2.h, s2.n)
                s.activation(A_t[:, :], A_t[:, :], AF.Exp)
            p.sec("scalar", f_s)
        with phase() as p:
            p.sec("vector", lambda v:
                  v.tensor_scalar_mul(A_t[:, :], A_t[:, :], -1.0))
        ps_at = psum("ps_at", [128, 128], F32, ini)
        for i in range(DIC * NSC):
            dc, h = divmod(i, NSC)
            with phase() as p:
                p.sec("tensor", lambda t, dc=dc, h=h:
                      t.transpose(ps_at[:, :],
                                  A_t[:, dc * NS + h * 128: dc * NS + (h + 1) * 128],
                                  ident[:, :]))
            with phase() as p:
                p.sec("vector", lambda v, dc=dc, h=h:
                      v.tensor_copy(AT[:, h * DI + dc * 128: h * DI + (dc + 1) * 128],
                                    ps_at[:, :]))
        ini.close()
        af.release(rowbuf)
        af.release(gw_rows)
        af.release(A_t)

        def emit_tokenwise(b, mode, dstT, tag, gam_bc=None, bet_bc=None):
            st = contextlib.ExitStack()
            xtm = (af.alloc(DM), af.alloc(DM))
            sqtm = (af.alloc(DM), af.alloc(DM))
            ytmp = sqtm
            if mode == "ln2":
                otm = (ab.alloc(DM), ab.alloc(DM))
                ps_tp = psum(f"pstp_{tag}", [128, DM], BF16, st)
            else:
                otm = ytmp
                ps_tp = psum(f"pstp_{tag}", [128, DM], F32, st)
            with st:
                with phase() as p:
                    b1, b3, b4, b6, b8 = s1.n, s3.n, s4.n, s6.n, s8.n

                    def f_sy(sy):
                        for i in range(16):
                            if i >= 2:
                                sy.wait_ge(s3.h, b3 + 3 * (i - 2) + 2)
                            sy.dma_start(out=xtm[i % 2][:, :],
                                         in_=x_in[b, i * 128:(i + 1) * 128, :]
                                         ).then_inc(s1.h, 16)
                    p.sec("sync", f_sy)
                    s1.n += 16 * 16

                    def f_s(s):
                        for i in range(16):
                            c0 = (i % 2) * 8
                            s.wait_ge(s1.h, b1 + 16 * (i + 1))
                            if i >= 2:
                                s.wait_ge(s3.h, b3 + 3 * (i - 2) + 2)
                                s.wait_ge(s8.h, b8 + (i - 2) + 1)
                            s.activation(sqtm[i % 2][:, :], xtm[i % 2][:, :],
                                         AF.Square).then_inc(s4.h, 1)
                            s.wait_ge(s3.h, b3 + 3 * i + 1)
                            adep(s, s.activation(statc[:, c0 + 2:c0 + 3],
                                                 statc[:, c0 + 6:c0 + 7], AF.Ln))
                            s.activation(statc[:, c0 + 7:c0 + 8],
                                         statc[:, c0 + 2:c0 + 3], AF.Exp,
                                         scale=-0.5).then_inc(s6.h, 1)
                    p.sec("scalar", f_s)
                    s4.n += 16
                    s6.n += 16

                    def f_v(v):
                        for i in range(16):
                            c0 = (i % 2) * 8
                            raws = statc[:, c0 + 0:c0 + 1]
                            mu = statc[:, c0 + 1:c0 + 2]
                            rawq = statc[:, c0 + 2:c0 + 3]
                            ex2 = statc[:, c0 + 3:c0 + 4]
                            musq = statc[:, c0 + 4:c0 + 5]
                            var = statc[:, c0 + 5:c0 + 6]
                            arg = statc[:, c0 + 6:c0 + 7]
                            sc = statc[:, c0 + 7:c0 + 8]
                            v.wait_ge(s1.h, b1 + 16 * (i + 1))
                            vdep(v, v.reduce_sum(raws, xtm[i % 2][:, :], axis=AX.X))
                            vdep(v, v.tensor_scalar_mul(mu, raws, 1.0 / DM))
                            v.wait_ge(s4.h, b4 + i + 1)
                            vdep(v, v.reduce_sum(rawq, sqtm[i % 2][:, :], axis=AX.X))
                            vdep(v, v.tensor_scalar_mul(ex2, rawq, 1.0 / DM))
                            vdep(v, v.tensor_mul(musq, mu, mu))
                            vdep(v, v.tensor_sub(var, ex2, musq))
                            if mode == "ln2":
                                ins = v.tensor_scalar(arg, var, 1.0 + EPS,
                                                      EPS * EPS, AL.mult, AL.add)
                            else:
                                ins = v.tensor_scalar_add(arg, var, EPS)
                            ins.then_inc(s3.h, 1)
                            v.wait_ge(s6.h, b6 + i + 1)
                            yt = ytmp[i % 2]
                            vdep(v, v.tensor_scalar(yt[:, :], xtm[i % 2][:, :],
                                                    mu, sc, AL.subtract, AL.mult))
                            if mode == "ln2":
                                vdep(v, v.tensor_mul(yt[:, :], yt[:, :],
                                                     gam_bc[:, 0:DM]))
                                v.tensor_add(otm[i % 2][:, :], yt[:, :],
                                             bet_bc[:, 0:DM]).then_inc(s3.h, 1)
                            else:
                                v.tensor_add(yt[:, :], yt[:, :],
                                             xtm[i % 2][:, :]).then_inc(s3.h, 1)
                            v.wait_ge(s8.h, b8 + i + 1)
                            dst = dstT.mkap(i * 128,
                                            [[dstT.arena.cap, 128], [T, DMC],
                                             [1, 128]])
                            v.tensor_copy(dst, ps_tp[:, :].rearrange(
                                "p (c n) -> p c n", c=DMC)).then_inc(s3.h, 1)
                    p.sec("vector", f_v)
                    s3.n += 3 * 16

                    def f_t(t):
                        idf = ident_bf if mode == "ln2" else ident
                        for i in range(16):
                            t.wait_ge(s3.h, b3 + 3 * i + 2)
                            if i >= 1:
                                t.wait_ge(s3.h, b3 + 3 * (i - 1) + 3)
                            ins = None
                            for dc in range(DMC):
                                ins = t.transpose(ps_tp[:, dc * 128:(dc + 1) * 128],
                                                  otm[i % 2][:, dc * 128:(dc + 1) * 128],
                                                  idf[:, :])
                            ins.then_inc(s8.h, 1)
                    p.sec("tensor", f_t)
                    s8.n += 16
            for v_ in xtm + sqtm:
                af.release(v_)
            if mode == "ln2":
                ab.release(otm[0])
                ab.release(otm[1])

        for b in range(NB):
            tb = f"b{b}"
            MILESTONES.append((pidx[0], f"p1_{tb}"))
            y2T = ab.alloc(DMC * T)
            W_up_bf = ab.alloc(DMC * DM)
            with phase() as p:
                def f_gp(gp):
                    gp.dma_start(out=W_up_bf[:, :],
                                 in_=gi["W_up"].rearrange("(c p) o -> p c o", p=128)
                                 ).then_inc(s2.h, 16)
                    gp.wait_ge(s2.h, s2.n + 16)
                p.sec("gpsimd", f_gp)
                s2.n += 16
            gam_bc = af.alloc(DM)
            bet_bc = af.alloc(DM)
            with phase() as p:
                def f_sy(sy):
                    for dstv, srcname in ((gam_bc, "ln_gamma"), (bet_bc, "ln_beta")):
                        base = s1.n
                        sy.dma_start(out=dstv[0:1, :],
                                     in_=gi[srcname].rearrange("(a d) -> a d", a=1)
                                     ).then_inc(s1.h, 16)
                        k, nd = 1, 1
                        while k < 128:
                            sy.wait_ge(s1.h, base + 16 * nd)
                            sy.dma_start(out=dstv[k:2 * k, 0:DM],
                                         in_=dstv[0:k, 0:DM]).then_inc(s1.h, 16)
                            k *= 2
                            nd += 1
                        sy.wait_ge(s1.h, base + 16 * 8)
                        s1.n += 16 * 8
                p.sec("sync", f_sy)
            emit_tokenwise(b, "ln2", y2T, f"p1{tb}", gam_bc, bet_bc)
            af.release(gam_bc)
            af.release(bet_bc)

            MILESTONES.append((pidx[0], f"p2_{tb}"))
            u_bf = ab.alloc(DIC * T)
            siluR = ab.alloc(DIC * T)
            BTt = ab.alloc(NSC * T)
            CTt = ab.alloc(NSC * T)
            W_delta_bf = ab.alloc(DIC * DI)
            W_B_bf = ab.alloc(DIC * NS)
            W_C_bf = ab.alloc(DIC * NS)
            with phase() as p:
                def f_gp(gp):
                    n0 = s2.n
                    for (dst, src) in [(W_delta_bf, gi["W_delta"]),
                                       (W_B_bf, gi["W_B"]), (W_C_bf, gi["W_C"])]:
                        gp.dma_start(out=dst[:, :],
                                     in_=src.rearrange("(c p) o -> p c o", p=128)
                                     ).then_inc(s2.h, 16)
                    gp.wait_ge(s2.h, n0 + 48)
                p.sec("gpsimd", f_gp)
                s2.n += 48

            Lcs = tuple(af.alloc(T) for _ in range(4))
            cvs = (af.alloc(T), af.alloc(T))
            with contextlib.ExitStack() as sA2:
                psu = (psum(f"psu0_{tb}", [128, 1024], F32, sA2),
                       psum(f"psu1_{tb}", [128, 1024], F32, sA2))

                with phase() as p:
                    b3, b4 = s3.n, s4.n

                    def f_t(t):
                        for do in range(DMC):
                            for th in range(2):
                                idx = do * 2 + th
                                if idx >= 2:
                                    t.wait_ge(s3.h, b3 + idx - 1)
                                ins = None
                                for c2 in range(2):
                                    for kc in range(DMC):
                                        ins = t.matmul(
                                            psu[idx % 2][:, c2 * 512:(c2 + 1) * 512],
                                            W_up_bf[:, kc * DM + do * 128:
                                                    kc * DM + (do + 1) * 128],
                                            y2T[:, kc * T + th * 1024 + c2 * 512:
                                                kc * T + th * 1024 + (c2 + 1) * 512],
                                            start=(kc == 0), stop=(kc == DMC - 1))
                                ins.then_inc(s4.h, 1)
                    p.sec("tensor", f_t)
                    s4.n += 16

                    def f_s(s):
                        for do in range(DMC):
                            for th in range(2):
                                idx = do * 2 + th
                                s.wait_ge(s4.h, b4 + idx + 1)
                                if do < DIC:
                                    s.activation(
                                        Lcs[do][:, th * 1024:(th + 1) * 1024],
                                        psu[idx % 2][:, :], AF.Identity,
                                        bias=bup_c[:, do:do + 1]).then_inc(s3.h, 1)
                                else:
                                    dc = do - DIC
                                    s.activation(
                                        siluR[:, dc * T + th * 1024:
                                              dc * T + (th + 1) * 1024],
                                        psu[idx % 2][:, :], AF.Silu,
                                        bias=bup_c[:, do:do + 1]).then_inc(s3.h, 1)
                    p.sec("scalar", f_s)
                    s3.n += 16
                ab.release(y2T)
                ab.release(W_up_bf)

                with phase() as p:
                    b3, b4 = s3.n, s4.n

                    def f_v(v):
                        for dc in range(DIC):
                            if dc >= 2:
                                v.wait_ge(s4.h, b4 + dc - 1)
                            L = Lcs[dc]
                            cv = cvs[dc % 2]
                            w0 = cw_cols[:, dc * 3 + 0: dc * 3 + 1]
                            w1 = cw_cols[:, dc * 3 + 1: dc * 3 + 2]
                            w2 = cw_cols[:, dc * 3 + 2: dc * 3 + 3]
                            vdep(v, v.memset(cv[:, 0:2], 0.0))
                            vdep(v, v.tensor_scalar_mul(cv[:, 2:T], L[:, 0:T - 2], w0))
                            vdep(v, v.scalar_tensor_tensor(cv[:, 1:T], L[:, 0:T - 1],
                                                           w1, cv[:, 1:T],
                                                           AL.mult, AL.add))
                            v.scalar_tensor_tensor(cv[:, 0:T], L[:, 0:T], w2,
                                                   cv[:, 0:T], AL.mult, AL.add
                                                   ).then_inc(s3.h, 1)
                    p.sec("vector", f_v)
                    s3.n += DIC

                    def f_s(s):
                        for dc in range(DIC):
                            s.wait_ge(s3.h, b3 + dc + 1)
                            s.activation(u_bf[:, dc * T:(dc + 1) * T],
                                         cvs[dc % 2][:, :], AF.Silu,
                                         bias=convb_c[:, dc:dc + 1]).then_inc(s4.h, 1)
                    p.sec("scalar", f_s)
                    s4.n += DIC

                with phase() as p:
                    b3, b4 = s3.n, s4.n

                    def f_t(t):
                        idx = 0
                        for wmat in (W_B_bf, W_C_bf):
                            for nh in range(NSC):
                                for th in range(2):
                                    if idx >= 2:
                                        t.wait_ge(s3.h, b3 + idx - 1)
                                    ins = None
                                    for c2 in range(2):
                                        for kc in range(DIC):
                                            ins = t.matmul(
                                                psu[idx % 2][:, c2 * 512:(c2 + 1) * 512],
                                                wmat[:, kc * NS + nh * 128:
                                                     kc * NS + (nh + 1) * 128],
                                                u_bf[:, kc * T + th * 1024 + c2 * 512:
                                                     kc * T + th * 1024 + (c2 + 1) * 512],
                                                start=(kc == 0), stop=(kc == DIC - 1))
                                    ins.then_inc(s4.h, 1)
                                    idx += 1
                    p.sec("tensor", f_t)
                    s4.n += 8

                    def f_v(v):
                        idx = 0
                        for dst in (BTt, CTt):
                            for nh in range(NSC):
                                for th in range(2):
                                    v.wait_ge(s4.h, b4 + idx + 1)
                                    v.tensor_copy(
                                        dst[:, nh * T + th * 1024:
                                            nh * T + (th + 1) * 1024],
                                        psu[idx % 2][:, :]).then_inc(s3.h, 1)
                                    idx += 1
                    p.sec("vector", f_v)
                    s3.n += 8
            for v_ in Lcs:
                af.release(v_)
            for v_ in cvs:
                af.release(v_)
            ab.release(W_B_bf)
            ab.release(W_C_bf)

            MILESTONES.append((pidx[0], f"p3_{tb}"))
            yscan = ab.alloc(DIC * T)
            dchunks = (af.alloc(T), af.alloc(T))
            uchunks = (ab.alloc(T), ab.alloc(T))
            dubc = (ab.alloc(T), ab.alloc(T))
            a_bufs = (af.alloc(NSC * T), af.alloc(NSC * T))
            x_t = ab.alloc(NSC * T)
            h_t = ab.alloc(NSC * T)
            g_t = ab.alloc(NSC * T)
            ystage = (ab.alloc(T, p=1),)
            tlns = (af.alloc(1024), af.alloc(1024))
            trl = af.alloc(1024)
            with contextlib.ExitStack() as sS:
                ps_y = psum(f"psy_{tb}", [1, T], F32, sS)

                def indcol(tensor_h, d):
                    base = tensor_h.ap()
                    return bass.AP(base.tensor, base.offset + d, [[128, 128], [0, 128]])

                for dc in range(DIC):
                    with contextlib.ExitStack() as sPrep:
                        ps_dp = (psum(f"psdp0_{tb}_{dc}", [128, 1024], F32, sPrep),
                                 psum(f"psdp1_{tb}_{dc}", [128, 1024], F32, sPrep))
                        with phase() as p:
                            def f_t(t, dc=dc, ps_dp=ps_dp):
                                for th in range(2):
                                    for c2 in range(2):
                                        for kc in range(DIC):
                                            t.matmul(ps_dp[th][:, c2 * 512:(c2 + 1) * 512],
                                                     W_delta_bf[:, kc * DI + dc * 128:
                                                                kc * DI + (dc + 1) * 128],
                                                     u_bf[:, kc * T + th * 1024 + c2 * 512:
                                                          kc * T + th * 1024 + (c2 + 1) * 512],
                                                     start=(kc == 0), stop=(kc == DIC - 1))
                            p.sec("tensor", f_t)
                        with phase() as p:
                            b4 = s4.n

                            def f_s(s, dc=dc, ps_dp=ps_dp):
                                for th in range(2):
                                    adep(s, s.activation(tlns[th][:, :], ps_dp[th][:, :],
                                                         AF.Abs,
                                                         bias=bdelta_c[:, dc:dc + 1]))
                                    adep(s, s.activation(tlns[th][:, :], tlns[th][:, :],
                                                         AF.Exp, scale=-1.0))
                                    s.activation(tlns[th][:, :], tlns[th][:, :],
                                                 AF.Ln, bias=1.0).then_inc(s4.h, 1)
                            p.sec("scalar", f_s)
                            s4.n += 2

                            def f_v(v, dc=dc, ps_dp=ps_dp):
                                dch = dchunks[dc % 2]
                                uch = uchunks[dc % 2]
                                for th in range(2):
                                    v.wait_ge(s4.h, b4 + th + 1)
                                    vdep(v, v.tensor_scalar(trl[:, :], ps_dp[th][:, :],
                                                            bdelta_c[:, dc:dc + 1], 0.0,
                                                            AL.add, AL.max))
                                    vdep(v, v.tensor_add(
                                        dch[:, th * 1024:(th + 1) * 1024],
                                        tlns[th][:, :], trl[:, :]))
                                    v.tensor_mul(uch[:, th * 1024:(th + 1) * 1024],
                                                 dch[:, th * 1024:(th + 1) * 1024],
                                                 u_bf[:, dc * T + th * 1024:
                                                      dc * T + (th + 1) * 1024])
                            p.sec("vector", f_v)

                    with contextlib.ExitStack() as sLoop:
                        ps_bc = psum(f"psbc_{tb}_{dc}", [128, T], F32, sLoop)
                        with phase() as p:
                            bTb = s1.n
                            bAe2 = s2.n
                            bStg = s5.n
                            bAe = s6.n
                            bVx = s3.n
                            bVg = s7.n
                            bTy = s8.n

                            def f_t(t, dc=dc, ps_bc=ps_bc):
                                dch = dchunks[dc % 2]
                                uch = uchunks[dc % 2]
                                for i in range(128):
                                    if i >= 1:
                                        t.wait_ge(s2.h, bAe2 + 2 * i)
                                    ins = None
                                    for c in range(4):
                                        ins = t.matmul(ps_bc[:, c * 512:(c + 1) * 512],
                                                       indcol(ident, i),
                                                       dch[:, c * 512:(c + 1) * 512],
                                                       start=True, stop=True)
                                    ins.then_inc(s1.h, 1)
                                    t.wait_ge(s2.h, bAe2 + 2 * i + 1)
                                    ins = None
                                    for c in range(4):
                                        ins = t.matmul(ps_bc[:, c * 512:(c + 1) * 512],
                                                       indcol(ident_bf, i),
                                                       uch[:, c * 512:(c + 1) * 512],
                                                       start=True, stop=True)
                                    ins.then_inc(s1.h, 1)
                                    t.wait_ge(s7.h, bVg + 2 * (i + 1))
                                    if i >= 1:
                                        t.wait_ge(s6.h, bAe + i)
                                    ins = None
                                    for c in range(4):
                                        for nh in range(NSC):
                                            ins = t.matmul(
                                                ps_y[0:1, c * 512:(c + 1) * 512],
                                                ones_bf[:, 0:1],
                                                g_t[:, nh * T + c * 512:
                                                    nh * T + (c + 1) * 512],
                                                start=(nh == 0), stop=(nh == NSC - 1))
                                    ins.then_inc(s8.h, 1)
                            p.sec("tensor", f_t)
                            s1.n += 256
                            s8.n += 128

                            def f_s(s, dc=dc, ps_bc=ps_bc):
                                for i in range(128):
                                    d = dc * 128 + i
                                    s.wait_ge(s1.h, bTb + 2 * i + 1)
                                    if i >= 2:
                                        s.wait_ge(s3.h, bVx + 4 * (i - 1))
                                    ins = None
                                    for nh in range(NSC):
                                        ins = s.activation(
                                            a_bufs[i % 2][:, nh * T:(nh + 1) * T],
                                            ps_bc[:, :], AF.Exp,
                                            scale=AT[:, nh * DI + d:nh * DI + d + 1])
                                    ins.then_inc(s2.h, 1)
                                    s.wait_ge(s1.h, bTb + 2 * i + 2)
                                    if i >= 2:
                                        s.wait_ge(s3.h, bVx + 4 * (i - 1) + 2)
                                    s.activation(dubc[i % 2][:, :], ps_bc[:, :],
                                                 AF.Identity).then_inc(s2.h, 1)
                                    if i >= 1:
                                        s.wait_ge(s8.h, bTy + i)
                                        if i >= 2:
                                            s.wait_ge(s5.h, bStg + 16 * (i - 1))
                                        s.activation(ystage[0][0:1, :],
                                                     ps_y[0:1, :], AF.Copy
                                                     ).then_inc(s6.h, 1)
                                s.wait_ge(s8.h, bTy + 128)
                                s.wait_ge(s5.h, bStg + 16 * 127)
                                s.activation(ystage[0][0:1, :], ps_y[0:1, :],
                                             AF.Copy).then_inc(s6.h, 1)
                            p.sec("scalar", f_s)
                            s2.n += 256
                            s6.n += 128

                            def f_v(v, dc=dc):
                                for i in range(128):
                                    v.wait_ge(s2.h, bAe2 + 2 * i + 2)
                                    for nh in range(NSC):
                                        v.tensor_mul(
                                            x_t[:, nh * T:(nh + 1) * T],
                                            BTt[:, nh * T:(nh + 1) * T],
                                            dubc[i % 2][:, :]).then_inc(s3.h, 1)
                                    v.wait_ge(s3.h, bVx + 4 * i + 2)
                                    v.wait_ge(s2.h, bAe2 + 2 * i + 1)
                                    if i >= 1:
                                        v.wait_ge(s7.h, bVg + 2 * i)
                                    for nh in range(NSC):
                                        v.tensor_tensor_scan(
                                            h_t[:, nh * T:(nh + 1) * T],
                                            a_bufs[i % 2][:, nh * T:(nh + 1) * T],
                                            x_t[:, nh * T:(nh + 1) * T],
                                            0.0, AL.mult, AL.add).then_inc(s3.h, 1)
                            p.sec("vector", f_v)
                            s3.n += 512

                            def f_gp(gp, dc=dc):
                                for i in range(128):
                                    gp.wait_ge(s3.h, bVx + 4 * (i + 1))
                                    if i >= 1:
                                        gp.wait_ge(s8.h, bTy + i)
                                    for nh in range(NSC):
                                        gp.tensor_mul(
                                            g_t[:, nh * T:(nh + 1) * T],
                                            h_t[:, nh * T:(nh + 1) * T],
                                            CTt[:, nh * T:(nh + 1) * T]
                                            ).then_inc(s7.h, 1)
                                    if i >= 1:
                                        gp.wait_ge(s6.h, bAe + i)
                                        gp.dma_start(
                                            out=yscan[i - 1:i, dc * T:(dc + 1) * T],
                                            in_=ystage[0][0:1, :]).then_inc(s5.h, 16)
                                gp.wait_ge(s6.h, bAe + 128)
                                gp.dma_start(out=yscan[127:128, dc * T:(dc + 1) * T],
                                             in_=ystage[0][0:1, :]).then_inc(s5.h, 16)
                                gp.wait_ge(s5.h, bStg + 16 * 128)
                            p.sec("gpsimd", f_gp)
                            s7.n += 256
                            s5.n += 16 * 128
            for v_ in (dchunks + uchunks + dubc + a_bufs +
                       (x_t, h_t, g_t) + ystage + tlns + (trl,)):
                (v_.arena).release(v_)
            ab.release(BTt)
            ab.release(CTt)
            ab.release(W_delta_bf)

            MILESTONES.append((pidx[0], f"p4_{tb}"))
            gated = ab.alloc(DIC * T)
            with phase() as p:
                def f_v(v):
                    for dc in range(DIC):
                        vdep(v, v.scalar_tensor_tensor(
                            gated[:, dc * T:(dc + 1) * T],
                            u_bf[:, dc * T:(dc + 1) * T],
                            dvec_c[:, dc:dc + 1],
                            yscan[:, dc * T:(dc + 1) * T], AL.mult, AL.add))
                        v.tensor_mul(gated[:, dc * T:(dc + 1) * T],
                                     gated[:, dc * T:(dc + 1) * T],
                                     siluR[:, dc * T:(dc + 1) * T])
                p.sec("vector", f_v)
            ab.release(u_bf)
            ab.release(siluR)
            ab.release(yscan)

            x_mid = af.alloc(DMC * T)
            W_down_bf = ab.alloc(DIC * DM)
            with phase() as p:
                def f_gp(gp):
                    gp.dma_start(out=W_down_bf[:, :],
                                 in_=gi["W_down"].rearrange("(c p) o -> p c o",
                                                            p=128)
                                 ).then_inc(s2.h, 16)
                    gp.wait_ge(s2.h, s2.n + 16)
                p.sec("gpsimd", f_gp)
                s2.n += 16
            emit_tokenwise(b, "resid", x_mid, f"p4{tb}")
            dnsc = (af.alloc(1024), af.alloc(1024))
            with contextlib.ExitStack() as sB2:
                psd = (psum(f"psd0_{tb}", [128, 1024], F32, sB2),
                       psum(f"psd1_{tb}", [128, 1024], F32, sB2))
                with phase() as p:
                    b3, b4, b6 = s3.n, s4.n, s6.n

                    def f_t(t):
                        for do in range(DMC):
                            for th in range(2):
                                idx = do * 2 + th
                                if idx >= 2:
                                    t.wait_ge(s6.h, b6 + idx - 1)
                                ins = None
                                for c2 in range(2):
                                    for kc in range(DIC):
                                        ins = t.matmul(
                                            psd[idx % 2][:, c2 * 512:(c2 + 1) * 512],
                                            W_down_bf[:, kc * DM + do * 128:
                                                      kc * DM + (do + 1) * 128],
                                            gated[:, kc * T + th * 1024 + c2 * 512:
                                                  kc * T + th * 1024 + (c2 + 1) * 512],
                                            start=(kc == 0), stop=(kc == DIC - 1))
                                ins.then_inc(s4.h, 1)
                    p.sec("tensor", f_t)
                    s4.n += 16

                    def f_s(s):
                        for do in range(DMC):
                            for th in range(2):
                                idx = do * 2 + th
                                s.wait_ge(s4.h, b4 + idx + 1)
                                if idx >= 2:
                                    s.wait_ge(s3.h, b3 + idx - 1)
                                s.activation(dnsc[idx % 2][:, :],
                                             psd[idx % 2][:, :], AF.Identity,
                                             bias=bdown_c[:, do:do + 1]
                                             ).then_inc(s6.h, 1)
                    p.sec("scalar", f_s)
                    s6.n += 16

                    def f_v(v):
                        for do in range(DMC):
                            for th in range(2):
                                idx = do * 2 + th
                                v.wait_ge(s6.h, b6 + idx + 1)
                                sl = slice(do * T + th * 1024,
                                           do * T + (th + 1) * 1024)
                                v.tensor_add(x_mid[:, sl], x_mid[:, sl],
                                             dnsc[idx % 2][:, :]).then_inc(s3.h, 1)
                    p.sec("vector", f_v)
                    s3.n += 16
            af.release(dnsc[0])
            af.release(dnsc[1])
            ab.release(gated)
            ab.release(W_down_bf)

            MILESTONES.append((pidx[0], f"p5_{tb}"))
            tokT = ab.alloc(DMC * T)
            hT = ab.alloc(32 * 1024)
            w1bufs = (ab.alloc(8 * 256), ab.alloc(8 * 256))
            w2bufs = (ab.alloc(16 * 128), ab.alloc(16 * 128))
            mu3 = af.alloc(T)
            r13 = af.alloc(T)
            sq3 = af.alloc(T)
            with contextlib.ExitStack() as sLs:
                ps_st = psum(f"ps_st3_{tb}", [128, T], F32, sLs)
                with phase() as p:
                    def f_t(t):
                        for dc in range(DMC):
                            for c in range(4):
                                t.matmul(ps_st[:, c * 512:(c + 1) * 512],
                                         ones_f[:, :],
                                         x_mid[:, dc * T + c * 512:
                                               dc * T + (c + 1) * 512],
                                         start=(dc == 0), stop=(dc == DMC - 1))
                    p.sec("tensor", f_t)
                with phase() as p:
                    p.sec("vector", lambda v:
                          v.tensor_scalar_mul(mu3[:, :], ps_st[:, :], 1.0 / DM))
                with phase() as p:
                    b3, b4 = s3.n, s4.n

                    def f_s(s):
                        for dc in range(DMC):
                            if dc >= 1:
                                s.wait_ge(s4.h, b4 + dc)
                            s.activation(sq3[:, :], x_mid[:, dc * T:(dc + 1) * T],
                                         AF.Square).then_inc(s3.h, 1)
                    p.sec("scalar", f_s)
                    s3.n += DMC

                    def f_t(t):
                        for dc in range(DMC):
                            t.wait_ge(s3.h, b3 + dc + 1)
                            ins = None
                            for c in range(4):
                                ins = t.matmul(ps_st[:, c * 512:(c + 1) * 512],
                                               ones_f[:, :],
                                               sq3[:, c * 512:(c + 1) * 512],
                                               start=(dc == 0), stop=(dc == DMC - 1))
                            ins.then_inc(s4.h, 1)
                    p.sec("tensor", f_t)
                    s4.n += DMC
                with phase() as p:
                    def f_v(v):
                        vdep(v, v.tensor_scalar_mul(r13[:, :], ps_st[:, :], 1.0 / DM))
                        vdep(v, v.tensor_mul(sq3[:, :], mu3[:, :], mu3[:, :]))
                        vdep(v, v.tensor_sub(r13[:, :], r13[:, :], sq3[:, :]))
                        v.tensor_scalar_add(r13[:, :], r13[:, :], EPS)
                    p.sec("vector", f_v)
                with phase() as p:
                    def f_s(s):
                        adep(s, s.activation(sq3[:, :], r13[:, :], AF.Ln))
                        s.activation(r13[:, :], sq3[:, :], AF.Exp, scale=-0.5)
                    p.sec("scalar", f_s)
                with phase() as p:
                    def f_v(v):
                        for dc in range(DMC):
                            vdep(v, v.tensor_sub(sq3[:, :],
                                                 x_mid[:, dc * T:(dc + 1) * T],
                                                 mu3[:, :]))
                            v.tensor_mul(tokT[:, dc * T:(dc + 1) * T],
                                         sq3[:, :], r13[:, :])
                    p.sec("vector", f_v)
            af.release(r13)
            af.release(sq3)
            mask_tm = af.alloc(16 * 8)
            mask_fm = af.alloc(T, p=8)

            with contextlib.ExitStack() as sL2:
                ps_l = psum(f"ps_l_{tb}", [128, 8], F32, sL2)
                ps_mf = psum(f"ps_mf_{tb}", [8, T], F32, sL2)
                with phase() as p:
                    b3, b4 = s3.n, s4.n

                    def f_t(t):
                        for tc in range(16):
                            if tc >= 1:
                                t.wait_ge(s3.h, b3 + tc)
                            for dc in range(DMC):
                                t.matmul(ps_l[:, 0:8],
                                         x_mid[:, dc * T + tc * 128:
                                               dc * T + (tc + 1) * 128],
                                         gwT[:, dc * 8:(dc + 1) * 8],
                                         start=(dc == 0), stop=False)
                            t.matmul(ps_l[:, 0:8],
                                     mu3[0:1, tc * 128:(tc + 1) * 128],
                                     ngs_row[0:1, 0:8], start=False, stop=True
                                     ).then_inc(s4.h, 1)
                    p.sec("tensor", f_t)
                    s4.n += 16

                    def f_v(v):
                        for tc in range(16):
                            v.wait_ge(s4.h, b4 + tc + 1)
                            vdep(v, v.reduce_max(rmax[:, 0:1], ps_l[:, 0:8],
                                                 axis=AX.X))
                            v.tensor_single_scalar(mask_tm[:, tc * 8:(tc + 1) * 8],
                                                   ps_l[:, 0:8], rmax[:, 0:1],
                                                   AL.is_ge).then_inc(s3.h, 1)
                    p.sec("vector", f_v)
                    s3.n += 16
                with phase() as p:
                    def f_t(t):
                        for tc in range(16):
                            t.transpose(ps_mf[0:8, tc * 128:(tc + 1) * 128],
                                        mask_tm[:, tc * 8:(tc + 1) * 8],
                                        ident[:, :])
                    p.sec("tensor", f_t)
                with phase() as p:
                    p.sec("vector", lambda v:
                          v.tensor_copy(mask_fm[0:8, :], ps_mf[0:8, :]))
            af.release(mu3)
            mask_bc = af.alloc(T)
            tmpo = af.alloc(1024)

            MILESTONES.append((pidx[0], f"p6_{tb}"))
            with contextlib.ExitStack() as sF:
                ps_h = (psum(f"psh0_{tb}", [128, 1024], F32, sF),
                        psum(f"psh1_{tb}", [128, 1024], F32, sF))
                ps_o = (psum(f"pso0_{tb}", [128, 1024], F32, sF),
                        psum(f"pso1_{tb}", [128, 1024], F32, sF))

                for e in range(NEXP):
                    with phase() as p:
                        def f_sy(sy, e=e):
                            base = s1.n
                            sy.dma_start(out=mask_bc[0:1, :],
                                         in_=mask_fm[e:e + 1, :]).then_inc(s1.h, 16)
                            k, nd = 1, 1
                            while k < 128:
                                sy.wait_ge(s1.h, base + 16 * nd)
                                sy.dma_start(out=mask_bc[k:2 * k, 0:T],
                                             in_=mask_bc[0:k, 0:T]).then_inc(s1.h, 16)
                                k *= 2
                                nd += 1
                            sy.wait_ge(s1.h, base + 16 * 8)
                        p.sec("sync", f_sy)
                        s1.n += 16 * 8

                    with phase() as p:
                        bW1, bW2 = s1.n, s2.n
                        bTh, bAg = s5.n, s6.n
                        bTo, bVo = s7.n, s8.n

                        def f_gp(gp, e=e):
                            for ts in range(2):
                                for fq in range(16):
                                    j = ts * 16 + fq
                                    if j >= 2:
                                        gp.wait_ge(s5.h, bTh + 2 * (j - 1))
                                    gp.dma_start(
                                        out=w1bufs[j % 2][:, :],
                                        in_=gi["w1"][e, :, fq * 256:(fq + 1) * 256]
                                        .rearrange("(c p) f -> p c f", p=128)
                                        ).then_inc(s1.h, 16)
                                for dh in range(16):
                                    j = ts * 16 + dh
                                    doc, hf = divmod(dh, 2)
                                    if j >= 2:
                                        gp.wait_ge(s7.h, bTo + (j - 1))
                                    gp.dma_start(
                                        out=w2bufs[j % 2][:, :],
                                        in_=gi["w2"][e, hf * 2048:(hf + 1) * 2048,
                                                     doc * 128:(doc + 1) * 128]
                                        .rearrange("(c p) o -> p c o", p=128)
                                        ).then_inc(s2.h, 16)
                        p.sec("gpsimd", f_gp)
                        s1.n += 16 * 32
                        s2.n += 16 * 32

                        def f_t(t, e=e):
                            for ts in range(2):
                                for fq in range(16):
                                    j = ts * 16 + fq
                                    t.wait_ge(s1.h, bW1 + 16 * (j + 1))
                                    for sub in range(2):
                                        fc = fq * 2 + sub
                                        jj = j * 2 + sub
                                        if jj >= 2:
                                            t.wait_ge(s6.h, bAg + (jj - 1))
                                        ins = None
                                        for c2 in range(2):
                                            for kc in range(DMC):
                                                ins = t.matmul(
                                                    ps_h[fc % 2][:, c2 * 512:(c2 + 1) * 512],
                                                    w1bufs[j % 2][:, kc * 256 + sub * 128:
                                                                  kc * 256 + (sub + 1) * 128],
                                                    tokT[:, kc * T + ts * 1024 + c2 * 512:
                                                         kc * T + ts * 1024 + (c2 + 1) * 512],
                                                    start=(kc == 0), stop=(kc == DMC - 1))
                                        ins.then_inc(s5.h, 1)
                                for dh in range(16):
                                    j = ts * 16 + dh
                                    doc, hf = divmod(dh, 2)
                                    t.wait_ge(s2.h, bW2 + 16 * (j + 1))
                                    t.wait_ge(s6.h, bAg + 32 * (ts + 1))
                                    if hf == 0 and ts * 8 + doc >= 2:
                                        t.wait_ge(s8.h, bVo + (ts * 8 + doc - 1))
                                    ins = None
                                    for c2 in range(2):
                                        for fc in range(16):
                                            gfc = hf * 16 + fc
                                            ins = t.matmul(
                                                ps_o[doc % 2][:, c2 * 512:(c2 + 1) * 512],
                                                w2bufs[j % 2][:, fc * 128:(fc + 1) * 128],
                                                hT[:, gfc * 1024 + c2 * 512:
                                                    gfc * 1024 + (c2 + 1) * 512],
                                                start=(gfc == 0), stop=(gfc == 31))
                                    ins.then_inc(s7.h, 1)
                        p.sec("tensor", f_t)
                        s5.n += 64
                        s7.n += 32

                        def f_s(s, e=e):
                            for ts in range(2):
                                for fc in range(32):
                                    j = ts * 32 + fc
                                    s.wait_ge(s5.h, bTh + j + 1)
                                    s.activation(hT[:, fc * 1024:(fc + 1) * 1024],
                                                 ps_h[fc % 2][:, :], AF.Gelu
                                                 ).then_inc(s6.h, 1)
                        p.sec("scalar", f_s)
                        s6.n += 64

                        def f_v(v, e=e):
                            for ts in range(2):
                                for doc in range(8):
                                    v.wait_ge(s7.h, bTo + ts * 16 + doc * 2 + 2)
                                    vdep(v, v.tensor_mul(
                                        tmpo[:, :], ps_o[doc % 2][:, :],
                                        mask_bc[:, ts * 1024:(ts + 1) * 1024]))
                                    v.tensor_add(
                                        x_mid[:, doc * T + ts * 1024:
                                              doc * T + (ts + 1) * 1024],
                                        x_mid[:, doc * T + ts * 1024:
                                              doc * T + (ts + 1) * 1024],
                                        tmpo[:, :]).then_inc(s8.h, 1)
                        p.sec("vector", f_v)
                        s8.n += 16
            for v_ in (tokT, hT, w1bufs[0], w1bufs[1], w2bufs[0], w2bufs[1]):
                ab.release(v_)
            for v_ in (mask_tm, mask_bc, mask_fm, tmpo):
                af.release(v_)

            MILESTONES.append((pidx[0], f"p7_{tb}"))
            otm = (af.alloc(DM), af.alloc(DM))
            with contextlib.ExitStack() as sO:
                ps_ot = psum(f"ps_ot_{tb}", [128, DM], F32, sO)
                with phase() as p:
                    b3, b4 = s3.n, s4.n
                    bSt = sOut.n

                    def f_t(t, b=b):
                        for i in range(16):
                            if i >= 1:
                                t.wait_ge(s3.h, b3 + i)
                            ins = None
                            for dc in range(DMC):
                                ins = t.transpose(ps_ot[:, dc * 128:(dc + 1) * 128],
                                                  x_mid[:, dc * T + i * 128:
                                                        dc * T + (i + 1) * 128],
                                                  ident[:, :])
                            ins.then_inc(s4.h, 1)
                    p.sec("tensor", f_t)
                    s4.n += 16

                    def f_v(v, b=b):
                        for i in range(16):
                            v.wait_ge(s4.h, b4 + i + 1)
                            if i >= 2:
                                v.wait_ge(sOut.h, bSt + 16 * (i - 1))
                            v.tensor_copy(otm[i % 2][:, :], ps_ot[:, :]
                                          ).then_inc(s3.h, 1)
                    p.sec("vector", f_v)
                    s3.n += 16

                    def f_sy(sy, b=b):
                        for i in range(16):
                            sy.wait_ge(s3.h, b3 + i + 1)
                            sy.dma_start(out=y_out[b, i * 128:(i + 1) * 128, :],
                                         in_=otm[i % 2][:, :]).then_inc(sOut.h, 16)
                        sy.wait_ge(sOut.h, bSt + 16 * 16)
                    p.sec("sync", f_sy)
                    sOut.n += 16 * 16
            af.release(otm[0])
            af.release(otm[1])
            af.release(x_mid)

        with phase() as p:
            p.sec("sync", lambda sy: sy.wait_ge(sOut.h, sOut.n))
    return nc


def kernel(**inputs):
    x = np.ascontiguousarray(inputs["L0_attn_x"])
    nm = {
        "ln_gamma": "L0_attn_ln_gamma", "ln_beta": "L0_attn_ln_beta",
        "W_up": "L0_attn_W_up", "b_up": "L0_attn_b_up",
        "conv_w": "L0_attn_conv_w", "conv_b": "L0_attn_conv_b",
        "W_delta": "L0_attn_W_delta", "b_delta": "L0_attn_b_delta",
        "W_B": "L0_attn_W_B", "W_C": "L0_attn_W_C",
        "A_log": "L0_attn_A_log", "Dvec": "L0_attn_D",
        "W_down": "L0_attn_W_down", "b_down": "L0_attn_b_down",
        "gate_w": "L0_ffn_gate_w", "w1": "L0_ffn_w1", "w2": "L0_ffn_w2",
    }
    weights = {k: np.ascontiguousarray(inputs[v]) for k, v in nm.items()}
    nc = build()
    in_maps = []
    for c in range(8):
        m = dict(weights)
        m["x_in"] = x[2 * c:2 * c + 2]
        in_maps.append(m)
    res = run_bass_kernel_spmd(nc, in_maps, core_ids=list(range(8)))
    return np.concatenate([r["y_out"] for r in res.results], axis=0)


# version 07
# speedup vs baseline: 1.9766x; 1.9766x over previous
import contextlib
import numpy as np

import concourse.bass as bass
import concourse.mybir as mybir
from concourse.bass_utils import run_bass_kernel_spmd

F32 = mybir.dt.float32
BF16 = mybir.dt.bfloat16
AL = mybir.AluOpType
AF = mybir.ActivationFunctionType
AX = mybir.AxisListType

NB = 2
T = 2048
DM = 1024
DI = 512
NS = 256
NEXP = 8
FF = 4096
EPS = 1e-5
DMC = DM // 128
DIC = DI // 128
NSC = NS // 128

ENGS = ("sync", "vector", "scalar", "tensor", "gpsimd")
PHASE_LIMIT = None
MILESTONES = []


class Sem:
    __slots__ = ("h", "n")

    def __init__(self, nc, es, name):
        self.h = es.enter_context(nc.semaphore(name))
        self.n = 0


class Phase:
    def __init__(self, nc, bar, idx):
        self.bar = bar
        self.idx = idx
        self.blk = nc.Block()
        self.done = set()

    def __enter__(self):
        self.blk.__enter__()
        return self

    def sec(self, eng_name, fn):
        assert eng_name not in self.done, eng_name
        self.done.add(eng_name)
        if PHASE_LIMIT is not None and self.idx >= PHASE_LIMIT:
            return
        target = 5 * self.idx
        dec = getattr(self.blk, eng_name)

        def wrapped(eng):
            eng.wait_ge(self.bar.h, target)
            fn(eng)
            eng.maybe_drain_then_inc((self.bar.h, 1))

        dec(wrapped)
        self.bar.n += 1

    def __exit__(self, *a):
        for e in ENGS:
            if e not in self.done:
                self.sec(e, lambda _eng: None)
        return self.blk.__exit__(*a)


class View:
    __slots__ = ("arena", "off", "n", "p")

    def __init__(self, arena, off, n, p=128):
        self.arena = arena
        self.off = off
        self.n = n
        self.p = p

    def __getitem__(self, key):
        ps, fs = key
        start = fs.start if fs.start is not None else 0
        stop = fs.stop if fs.stop is not None else self.n
        assert 0 <= start <= stop <= self.n, (start, stop, self.n)
        return self.arena.t[ps, self.off + start: self.off + stop]

    def mkap(self, extra_off, dims):
        base = self.arena.t.ap()
        return bass.AP(base.tensor, base.offset + self.off + extra_off, dims)


class Arena:
    def __init__(self, nc, es, name, dtype, nelem):
        self.t = es.enter_context(nc.sbuf_tensor(name, [128, nelem], dtype))
        self.cap = nelem
        self.free = [(0, nelem)]
        self.name = name

    def alloc(self, nelem, p=128):
        nelem_a = (nelem + 127) & ~127
        for i, (off, sz) in enumerate(self.free):
            if sz >= nelem_a:
                if sz == nelem_a:
                    self.free.pop(i)
                else:
                    self.free[i] = (off + nelem_a, sz - nelem_a)
                return View(self, off, nelem, p)
        raise MemoryError(f"arena {self.name}: need {nelem_a}, free={self.free}")

    def release(self, view):
        nelem_a = (view.n + 127) & ~127
        self.free.append((view.off, nelem_a))
        self.free.sort()
        merged = []
        for off, sz in self.free:
            if merged and merged[-1][0] + merged[-1][1] == off:
                merged[-1] = (merged[-1][0], merged[-1][1] + sz)
            else:
                merged.append((off, sz))
        self.free = [tuple(x) for x in merged]


def build():
    nc = bass.Bass("TRN2", target_bir_lowering=False)

    x_in = nc.dram_tensor("x_in", [NB, T, DM], F32, kind="ExternalInput").ap()
    gi = {}
    for name, shape in [
        ("ln_gamma", [DM]), ("ln_beta", [DM]), ("W_up", [DM, DM]),
        ("b_up", [DM]), ("conv_w", [DI, 1, 3]), ("conv_b", [DI]),
        ("W_delta", [DI, DI]), ("b_delta", [DI]), ("W_B", [DI, NS]),
        ("W_C", [DI, NS]), ("A_log", [DI, NS]), ("Dvec", [DI]),
        ("W_down", [DI, DM]), ("b_down", [DM]), ("gate_w", [NEXP, DM]),
        ("w1", [NEXP, DM, FF]), ("w2", [NEXP, FF, DM]),
    ]:
        gi[name] = nc.dram_tensor(name, shape, F32, kind="ExternalInput").ap()
    y_out = nc.dram_tensor("y_out", [NB, T, DM], F32, kind="ExternalOutput").ap()

    es = contextlib.ExitStack()
    with es:
        def sb(name, shape, dt, stack=None):
            return (stack or es).enter_context(nc.sbuf_tensor(name, shape, dt))

        def psum(name, shape, dt, stack=None):
            return (stack or es).enter_context(nc.psum_tensor(name, shape, dt))

        ident = sb("ident", [128, 128], F32)
        ident_bf = sb("ident_bf", [128, 128], BF16)
        ones_f = sb("ones_f", [128, 128], F32)
        ones_bf = sb("ones_bf", [128, 1], BF16)
        cw_cols = sb("cw_cols", [128, DIC * 3], F32)
        bup_c = sb("bup_c", [128, DMC], F32)
        convb_c = sb("convb_c", [128, DIC], F32)
        bdelta_c = sb("bdelta_c", [128, DIC], F32)
        dvec_c = sb("dvec_c", [128, DIC], F32)
        bdown_c = sb("bdown_c", [128, DMC], F32)
        AT = sb("AT", [128, NSC * DI], F32)
        gwT = sb("gwT", [128, DMC * 8], F32)
        ngs_row = sb("ngs_row", [1, 8], F32)
        statc = sb("statc", [128, 16], F32)
        rmax = sb("rmax", [128, 1], F32)

        af = Arena(nc, es, "af", F32, 22528)
        ab = Arena(nc, es, "ab", BF16, 57344)

        bar = Sem(nc, es, "bar")
        s1 = Sem(nc, es, "s1")
        s2 = Sem(nc, es, "s2")
        s3 = Sem(nc, es, "s3")
        s4 = Sem(nc, es, "s4")
        s5 = Sem(nc, es, "s5")
        s6 = Sem(nc, es, "s6")
        s7 = Sem(nc, es, "s7")
        s8 = Sem(nc, es, "s8")
        sOut = Sem(nc, es, "sOut")
        sVd = Sem(nc, es, "sVd")
        sAd = Sem(nc, es, "sAd")

        def vdep(v, ins):
            sVd.n += 1
            ins.then_inc(sVd.h, 1)
            v.wait_ge(sVd.h, sVd.n)

        def adep(s, ins):
            sAd.n += 1
            ins.then_inc(sAd.h, 1)
            s.wait_ge(sAd.h, sAd.n)

        pidx = [0]

        def phase():
            p = Phase(nc, bar, pidx[0])
            pidx[0] += 1
            return p

        ini = contextlib.ExitStack()
        iot = sb("iot", [128, 128], F32, ini)
        rowbuf = af.alloc(1024, p=1)
        gw_rows = af.alloc(DM, p=8)
        A_t = af.alloc(DIC * NS)

        with phase() as p:
            def f_gp(gp):
                gp.iota(iot[:, 0:128], [[1, 128]], channel_multiplier=-1,
                        allow_small_or_imprecise_dtypes=True).then_inc(s2.h, 1)
                s2.n += 1
                gp.dma_start(out=A_t[:, :],
                             in_=gi["A_log"].rearrange("(c p) n -> p c n", p=128)
                             ).then_inc(s2.h, 16)
                s2.n += 16
                gp.dma_start(out=cw_cols[:, :],
                             in_=gi["conv_w"].rearrange("(c p) u k -> p c u k",
                                                        p=128)
                             ).then_inc(s2.h, 16)
                s2.n += 16
            p.sec("gpsimd", f_gp)

            def f_v(v):
                v.wait_ge(s2.h, 1)
                vdep(v, v.tensor_single_scalar(ident[:, :], iot[:, 0:128],
                                               0.0, AL.is_equal))
                v.tensor_copy(ident_bf[:, :], ident[:, :])
                v.memset(ones_f[:, :], 1.0)
                v.memset(ones_bf[:, :], 1.0)
            p.sec("vector", f_v)

        vec_loads = [
            ("b_up", bup_c, DMC), ("conv_b", convb_c, DIC),
            ("b_delta", bdelta_c, DIC), ("Dvec", dvec_c, DIC),
            ("b_down", bdown_c, DMC),
        ]
        ps_col = psum("ps_col", [128, 16], F32, ini)
        for (srcname, dst, nch) in vec_loads:
            src = gi[srcname]
            with phase() as p:
                def f_sy(sy, src=src):
                    sy.dma_start(out=rowbuf[0:1, 0:src.shape[0]],
                                 in_=src.rearrange("(a d) -> a d", a=1)
                                 ).then_inc(s1.h, 16)
                    sy.wait_ge(s1.h, s1.n + 16)
                p.sec("sync", f_sy)
                s1.n += 16
            with phase() as p:
                def f_t(t, nch=nch):
                    for c in range(nch):
                        t.transpose(ps_col[:, c:c + 1],
                                    rowbuf[0:1, c * 128:(c + 1) * 128],
                                    ident[0:1, 0:1])
                p.sec("tensor", f_t)
            with phase() as p:
                p.sec("vector", lambda v, dst=dst, nch=nch:
                      v.tensor_copy(dst[:, 0:nch], ps_col[:, 0:nch]))

        ps_ngs = psum("ps_ngs", [1, 8], F32, ini)
        with phase() as p:
            def f_sy(sy):
                sy.dma_start(out=gw_rows[0:8, :], in_=gi["gate_w"][:, :]
                             ).then_inc(s1.h, 16)
                sy.wait_ge(s1.h, s1.n + 16)
            p.sec("sync", f_sy)
            s1.n += 16
        for dc in range(DMC):
            with phase() as p:
                p.sec("tensor", lambda t, dc=dc:
                      t.transpose(ps_col[:, 0:8],
                                  gw_rows[0:8, dc * 128:(dc + 1) * 128],
                                  ident[0:8, 0:8]))
            with phase() as p:
                p.sec("vector", lambda v, dc=dc:
                      v.tensor_copy(gwT[:, dc * 8:(dc + 1) * 8], ps_col[:, 0:8]))
        with phase() as p:
            def f_t(t):
                for dc in range(DMC):
                    t.matmul(ps_ngs[0:1, 0:8], ones_f[:, 0:1],
                             gwT[:, dc * 8:(dc + 1) * 8],
                             start=(dc == 0), stop=(dc == DMC - 1))
            p.sec("tensor", f_t)
        with phase() as p:
            p.sec("vector", lambda v:
                  v.tensor_scalar_mul(ngs_row[0:1, :], ps_ngs[0:1, :], -1.0))

        with phase() as p:
            def f_s(s):
                s.wait_ge(s2.h, s2.n)
                s.activation(A_t[:, :], A_t[:, :], AF.Exp)
            p.sec("scalar", f_s)
        with phase() as p:
            p.sec("vector", lambda v:
                  v.tensor_scalar_mul(A_t[:, :], A_t[:, :], -1.0))
        ps_at = psum("ps_at", [128, 128], F32, ini)
        for i in range(DIC * NSC):
            dc, h = divmod(i, NSC)
            with phase() as p:
                p.sec("tensor", lambda t, dc=dc, h=h:
                      t.transpose(ps_at[:, :],
                                  A_t[:, dc * NS + h * 128: dc * NS + (h + 1) * 128],
                                  ident[:, :]))
            with phase() as p:
                p.sec("vector", lambda v, dc=dc, h=h:
                      v.tensor_copy(AT[:, h * DI + dc * 128: h * DI + (dc + 1) * 128],
                                    ps_at[:, :]))
        ini.close()
        af.release(rowbuf)
        af.release(gw_rows)
        af.release(A_t)

        def emit_tokenwise(b, mode, dstT, tag, gam_bc=None, bet_bc=None):
            st = contextlib.ExitStack()
            xtm = (af.alloc(DM), af.alloc(DM))
            sqtm = (af.alloc(DM), af.alloc(DM))
            ytmp = sqtm
            if mode == "ln2":
                otm = (ab.alloc(DM), ab.alloc(DM))
                ps_tp = psum(f"pstp_{tag}", [128, DM], BF16, st)
            else:
                otm = ytmp
                ps_tp = psum(f"pstp_{tag}", [128, DM], F32, st)
            with st:
                with phase() as p:
                    b1, b3, b4, b6, b8 = s1.n, s3.n, s4.n, s6.n, s8.n

                    def f_sy(sy):
                        for i in range(16):
                            if i >= 2:
                                sy.wait_ge(s3.h, b3 + 3 * (i - 2) + 2)
                            sy.dma_start(out=xtm[i % 2][:, :],
                                         in_=x_in[b, i * 128:(i + 1) * 128, :]
                                         ).then_inc(s1.h, 16)
                    p.sec("sync", f_sy)
                    s1.n += 16 * 16

                    def f_s(s):
                        for i in range(16):
                            c0 = (i % 2) * 8
                            s.wait_ge(s1.h, b1 + 16 * (i + 1))
                            if i >= 2:
                                s.wait_ge(s3.h, b3 + 3 * (i - 2) + 2)
                                s.wait_ge(s8.h, b8 + (i - 2) + 1)
                            s.activation(sqtm[i % 2][:, :], xtm[i % 2][:, :],
                                         AF.Square).then_inc(s4.h, 1)
                            s.wait_ge(s3.h, b3 + 3 * i + 1)
                            adep(s, s.activation(statc[:, c0 + 2:c0 + 3],
                                                 statc[:, c0 + 6:c0 + 7], AF.Ln))
                            s.activation(statc[:, c0 + 7:c0 + 8],
                                         statc[:, c0 + 2:c0 + 3], AF.Exp,
                                         scale=-0.5).then_inc(s6.h, 1)
                    p.sec("scalar", f_s)
                    s4.n += 16
                    s6.n += 16

                    def f_v(v):
                        for i in range(16):
                            c0 = (i % 2) * 8
                            raws = statc[:, c0 + 0:c0 + 1]
                            mu = statc[:, c0 + 1:c0 + 2]
                            rawq = statc[:, c0 + 2:c0 + 3]
                            ex2 = statc[:, c0 + 3:c0 + 4]
                            musq = statc[:, c0 + 4:c0 + 5]
                            var = statc[:, c0 + 5:c0 + 6]
                            arg = statc[:, c0 + 6:c0 + 7]
                            sc = statc[:, c0 + 7:c0 + 8]
                            v.wait_ge(s1.h, b1 + 16 * (i + 1))
                            vdep(v, v.reduce_sum(raws, xtm[i % 2][:, :], axis=AX.X))
                            vdep(v, v.tensor_scalar_mul(mu, raws, 1.0 / DM))
                            v.wait_ge(s4.h, b4 + i + 1)
                            vdep(v, v.reduce_sum(rawq, sqtm[i % 2][:, :], axis=AX.X))
                            vdep(v, v.tensor_scalar_mul(ex2, rawq, 1.0 / DM))
                            vdep(v, v.tensor_mul(musq, mu, mu))
                            vdep(v, v.tensor_sub(var, ex2, musq))
                            if mode == "ln2":
                                ins = v.tensor_scalar(arg, var, 1.0 + EPS,
                                                      EPS * EPS, AL.mult, AL.add)
                            else:
                                ins = v.tensor_scalar_add(arg, var, EPS)
                            ins.then_inc(s3.h, 1)
                            v.wait_ge(s6.h, b6 + i + 1)
                            yt = ytmp[i % 2]
                            vdep(v, v.tensor_scalar(yt[:, :], xtm[i % 2][:, :],
                                                    mu, sc, AL.subtract, AL.mult))
                            if mode == "ln2":
                                vdep(v, v.tensor_mul(yt[:, :], yt[:, :],
                                                     gam_bc[:, 0:DM]))
                                v.tensor_add(otm[i % 2][:, :], yt[:, :],
                                             bet_bc[:, 0:DM]).then_inc(s3.h, 1)
                            else:
                                v.tensor_add(yt[:, :], yt[:, :],
                                             xtm[i % 2][:, :]).then_inc(s3.h, 1)
                            v.wait_ge(s8.h, b8 + i + 1)
                            dst = dstT.mkap(i * 128,
                                            [[dstT.arena.cap, 128], [T, DMC],
                                             [1, 128]])
                            v.tensor_copy(dst, ps_tp[:, :].rearrange(
                                "p (c n) -> p c n", c=DMC)).then_inc(s3.h, 1)
                    p.sec("vector", f_v)
                    s3.n += 3 * 16

                    def f_t(t):
                        idf = ident_bf if mode == "ln2" else ident
                        for i in range(16):
                            t.wait_ge(s3.h, b3 + 3 * i + 2)
                            if i >= 1:
                                t.wait_ge(s3.h, b3 + 3 * (i - 1) + 3)
                            ins = None
                            for dc in range(DMC):
                                ins = t.transpose(ps_tp[:, dc * 128:(dc + 1) * 128],
                                                  otm[i % 2][:, dc * 128:(dc + 1) * 128],
                                                  idf[:, :])
                            ins.then_inc(s8.h, 1)
                    p.sec("tensor", f_t)
                    s8.n += 16
            for v_ in xtm + sqtm:
                af.release(v_)
            if mode == "ln2":
                ab.release(otm[0])
                ab.release(otm[1])

        for b in range(NB):
            tb = f"b{b}"
            MILESTONES.append((pidx[0], f"p1_{tb}"))
            y2T = ab.alloc(DMC * T)
            W_up_bf = ab.alloc(DMC * DM)
            with phase() as p:
                def f_gp(gp):
                    gp.dma_start(out=W_up_bf[:, :],
                                 in_=gi["W_up"].rearrange("(c p) o -> p c o", p=128)
                                 ).then_inc(s2.h, 16)
                    gp.wait_ge(s2.h, s2.n + 16)
                p.sec("gpsimd", f_gp)
                s2.n += 16
            gam_bc = af.alloc(DM)
            bet_bc = af.alloc(DM)
            with phase() as p:
                def f_sy(sy):
                    for dstv, srcname in ((gam_bc, "ln_gamma"), (bet_bc, "ln_beta")):
                        base = s1.n
                        sy.dma_start(out=dstv[0:1, :],
                                     in_=gi[srcname].rearrange("(a d) -> a d", a=1)
                                     ).then_inc(s1.h, 16)
                        k, nd = 1, 1
                        while k < 128:
                            sy.wait_ge(s1.h, base + 16 * nd)
                            sy.dma_start(out=dstv[k:2 * k, 0:DM],
                                         in_=dstv[0:k, 0:DM]).then_inc(s1.h, 16)
                            k *= 2
                            nd += 1
                        sy.wait_ge(s1.h, base + 16 * 8)
                        s1.n += 16 * 8
                p.sec("sync", f_sy)
            emit_tokenwise(b, "ln2", y2T, f"p1{tb}", gam_bc, bet_bc)
            af.release(gam_bc)
            af.release(bet_bc)

            MILESTONES.append((pidx[0], f"p2_{tb}"))
            u_bf = ab.alloc(DIC * T)
            siluR = ab.alloc(DIC * T)
            BTt = ab.alloc(NSC * T)
            CTt = ab.alloc(NSC * T)
            W_delta_bf = ab.alloc(DIC * DI)
            W_B_bf = ab.alloc(DIC * NS)
            W_C_bf = ab.alloc(DIC * NS)
            with phase() as p:
                def f_gp(gp):
                    n0 = s2.n
                    for (dst, src) in [(W_delta_bf, gi["W_delta"]),
                                       (W_B_bf, gi["W_B"]), (W_C_bf, gi["W_C"])]:
                        gp.dma_start(out=dst[:, :],
                                     in_=src.rearrange("(c p) o -> p c o", p=128)
                                     ).then_inc(s2.h, 16)
                    gp.wait_ge(s2.h, n0 + 48)
                p.sec("gpsimd", f_gp)
                s2.n += 48

            Lcs = tuple(af.alloc(T) for _ in range(4))
            cvs = (af.alloc(T), af.alloc(T))
            with contextlib.ExitStack() as sA2:
                psu = (psum(f"psu0_{tb}", [128, 1024], F32, sA2),
                       psum(f"psu1_{tb}", [128, 1024], F32, sA2))

                with phase() as p:
                    b3, b4 = s3.n, s4.n

                    def f_t(t):
                        for do in range(DMC):
                            for th in range(2):
                                idx = do * 2 + th
                                if idx >= 2:
                                    t.wait_ge(s3.h, b3 + idx - 1)
                                ins = None
                                for c2 in range(2):
                                    for kc in range(DMC):
                                        ins = t.matmul(
                                            psu[idx % 2][:, c2 * 512:(c2 + 1) * 512],
                                            W_up_bf[:, kc * DM + do * 128:
                                                    kc * DM + (do + 1) * 128],
                                            y2T[:, kc * T + th * 1024 + c2 * 512:
                                                kc * T + th * 1024 + (c2 + 1) * 512],
                                            start=(kc == 0), stop=(kc == DMC - 1))
                                ins.then_inc(s4.h, 1)
                    p.sec("tensor", f_t)
                    s4.n += 16

                    def f_s(s):
                        for do in range(DMC):
                            for th in range(2):
                                idx = do * 2 + th
                                s.wait_ge(s4.h, b4 + idx + 1)
                                if do < DIC:
                                    s.activation(
                                        Lcs[do][:, th * 1024:(th + 1) * 1024],
                                        psu[idx % 2][:, :], AF.Identity,
                                        bias=bup_c[:, do:do + 1]).then_inc(s3.h, 1)
                                else:
                                    dc = do - DIC
                                    s.activation(
                                        siluR[:, dc * T + th * 1024:
                                              dc * T + (th + 1) * 1024],
                                        psu[idx % 2][:, :], AF.Silu,
                                        bias=bup_c[:, do:do + 1]).then_inc(s3.h, 1)
                    p.sec("scalar", f_s)
                    s3.n += 16
                ab.release(y2T)
                ab.release(W_up_bf)

                with phase() as p:
                    b3, b4 = s3.n, s4.n

                    def f_v(v):
                        for dc in range(DIC):
                            if dc >= 2:
                                v.wait_ge(s4.h, b4 + dc - 1)
                            L = Lcs[dc]
                            cv = cvs[dc % 2]
                            w0 = cw_cols[:, dc * 3 + 0: dc * 3 + 1]
                            w1 = cw_cols[:, dc * 3 + 1: dc * 3 + 2]
                            w2 = cw_cols[:, dc * 3 + 2: dc * 3 + 3]
                            vdep(v, v.memset(cv[:, 0:2], 0.0))
                            vdep(v, v.tensor_scalar_mul(cv[:, 2:T], L[:, 0:T - 2], w0))
                            vdep(v, v.scalar_tensor_tensor(cv[:, 1:T], L[:, 0:T - 1],
                                                           w1, cv[:, 1:T],
                                                           AL.mult, AL.add))
                            v.scalar_tensor_tensor(cv[:, 0:T], L[:, 0:T], w2,
                                                   cv[:, 0:T], AL.mult, AL.add
                                                   ).then_inc(s3.h, 1)
                    p.sec("vector", f_v)
                    s3.n += DIC

                    def f_s(s):
                        for dc in range(DIC):
                            s.wait_ge(s3.h, b3 + dc + 1)
                            s.activation(u_bf[:, dc * T:(dc + 1) * T],
                                         cvs[dc % 2][:, :], AF.Silu,
                                         bias=convb_c[:, dc:dc + 1]).then_inc(s4.h, 1)
                    p.sec("scalar", f_s)
                    s4.n += DIC

                with phase() as p:
                    b3, b4 = s3.n, s4.n

                    def f_t(t):
                        idx = 0
                        for wmat in (W_B_bf, W_C_bf):
                            for nh in range(NSC):
                                for th in range(2):
                                    if idx >= 2:
                                        t.wait_ge(s3.h, b3 + idx - 1)
                                    ins = None
                                    for c2 in range(2):
                                        for kc in range(DIC):
                                            ins = t.matmul(
                                                psu[idx % 2][:, c2 * 512:(c2 + 1) * 512],
                                                wmat[:, kc * NS + nh * 128:
                                                     kc * NS + (nh + 1) * 128],
                                                u_bf[:, kc * T + th * 1024 + c2 * 512:
                                                     kc * T + th * 1024 + (c2 + 1) * 512],
                                                start=(kc == 0), stop=(kc == DIC - 1))
                                    ins.then_inc(s4.h, 1)
                                    idx += 1
                    p.sec("tensor", f_t)
                    s4.n += 8

                    def f_v(v):
                        idx = 0
                        for dst in (BTt, CTt):
                            for nh in range(NSC):
                                for th in range(2):
                                    v.wait_ge(s4.h, b4 + idx + 1)
                                    v.tensor_copy(
                                        dst[:, nh * T + th * 1024:
                                            nh * T + (th + 1) * 1024],
                                        psu[idx % 2][:, :]).then_inc(s3.h, 1)
                                    idx += 1
                    p.sec("vector", f_v)
                    s3.n += 8
            for v_ in Lcs:
                af.release(v_)
            for v_ in cvs:
                af.release(v_)
            ab.release(W_B_bf)
            ab.release(W_C_bf)

            MILESTONES.append((pidx[0], f"p3_{tb}"))
            yscan = ab.alloc(DIC * T)
            dchunks = (af.alloc(T), af.alloc(T))
            uchunks = (ab.alloc(T), ab.alloc(T))
            dubc = (ab.alloc(T), ab.alloc(T))
            a_bufs = (af.alloc(NSC * T), af.alloc(NSC * T))
            x_t = ab.alloc(NSC * T)
            h_t = ab.alloc(NSC * T)
            g_t = ab.alloc(NSC * T)
            ystage = (ab.alloc(T, p=1),)
            tlns = (af.alloc(1024), af.alloc(1024))
            trl = af.alloc(1024)
            with contextlib.ExitStack() as sS:
                ps_y = psum(f"psy_{tb}", [1, T], F32, sS)

                def indcol(tensor_h, d):
                    base = tensor_h.ap()
                    return bass.AP(base.tensor, base.offset + d, [[128, 128], [0, 128]])

                for dc in range(DIC):
                    with contextlib.ExitStack() as sPrep:
                        ps_dp = (psum(f"psdp0_{tb}_{dc}", [128, 1024], F32, sPrep),
                                 psum(f"psdp1_{tb}_{dc}", [128, 1024], F32, sPrep))
                        with phase() as p:
                            def f_t(t, dc=dc, ps_dp=ps_dp):
                                for th in range(2):
                                    for c2 in range(2):
                                        for kc in range(DIC):
                                            t.matmul(ps_dp[th][:, c2 * 512:(c2 + 1) * 512],
                                                     W_delta_bf[:, kc * DI + dc * 128:
                                                                kc * DI + (dc + 1) * 128],
                                                     u_bf[:, kc * T + th * 1024 + c2 * 512:
                                                          kc * T + th * 1024 + (c2 + 1) * 512],
                                                     start=(kc == 0), stop=(kc == DIC - 1))
                            p.sec("tensor", f_t)
                        with phase() as p:
                            b4 = s4.n

                            def f_s(s, dc=dc, ps_dp=ps_dp):
                                for th in range(2):
                                    adep(s, s.activation(tlns[th][:, :], ps_dp[th][:, :],
                                                         AF.Abs,
                                                         bias=bdelta_c[:, dc:dc + 1]))
                                    adep(s, s.activation(tlns[th][:, :], tlns[th][:, :],
                                                         AF.Exp, scale=-1.0))
                                    s.activation(tlns[th][:, :], tlns[th][:, :],
                                                 AF.Ln, bias=1.0).then_inc(s4.h, 1)
                            p.sec("scalar", f_s)
                            s4.n += 2

                            def f_v(v, dc=dc, ps_dp=ps_dp):
                                dch = dchunks[dc % 2]
                                uch = uchunks[dc % 2]
                                for th in range(2):
                                    v.wait_ge(s4.h, b4 + th + 1)
                                    vdep(v, v.tensor_scalar(trl[:, :], ps_dp[th][:, :],
                                                            bdelta_c[:, dc:dc + 1], 0.0,
                                                            AL.add, AL.max))
                                    vdep(v, v.tensor_add(
                                        dch[:, th * 1024:(th + 1) * 1024],
                                        tlns[th][:, :], trl[:, :]))
                                    v.tensor_mul(uch[:, th * 1024:(th + 1) * 1024],
                                                 dch[:, th * 1024:(th + 1) * 1024],
                                                 u_bf[:, dc * T + th * 1024:
                                                      dc * T + (th + 1) * 1024])
                            p.sec("vector", f_v)

                    with contextlib.ExitStack() as sLoop:
                        ps_del = psum(f"psdel_{tb}_{dc}", [128, 1024], F32, sLoop)
                        ps_du = psum(f"psdu_{tb}_{dc}", [128, 1024], F32, sLoop)
                        with phase() as p:
                            bTb = s4.n
                            bAa = s2.n
                            bStg = s5.n
                            bAe = s6.n
                            bVx = s3.n
                            bVg = s7.n
                            bTy = s8.n

                            def f_t(t, dc=dc):
                                dch = dchunks[dc % 2]
                                uch = uchunks[dc % 2]

                                def ymm(j):
                                    t.wait_ge(s7.h, bVg + 2 * (j + 1))
                                    if j >= 1:
                                        t.wait_ge(s6.h, bAe + j)
                                    ins = None
                                    for c in range(4):
                                        for nh in range(NSC):
                                            ins = t.matmul(
                                                ps_y[0:1, c * 512:(c + 1) * 512],
                                                ones_bf[:, 0:1],
                                                g_t[:, nh * T + c * 512:
                                                    nh * T + (c + 1) * 512],
                                                start=(nh == 0), stop=(nh == NSC - 1))
                                    ins.then_inc(s8.h, 1)

                                for i in range(128):
                                    if i >= 2:
                                        ymm(i - 2)
                                    if i >= 1:
                                        t.wait_ge(s2.h, bAa + 6 * (i - 1) + 5)
                                    ins = None
                                    for c in range(2):
                                        ins = t.matmul(ps_del[:, c * 512:(c + 1) * 512],
                                                       indcol(ident, i),
                                                       dch[:, c * 512:(c + 1) * 512],
                                                       start=True, stop=True)
                                    ins.then_inc(s4.h, 1)
                                    if i >= 1:
                                        t.wait_ge(s2.h, bAa + 6 * (i - 1) + 6)
                                    ins = None
                                    for c in range(2):
                                        ins = t.matmul(ps_du[:, c * 512:(c + 1) * 512],
                                                       indcol(ident_bf, i),
                                                       uch[:, c * 512:(c + 1) * 512],
                                                       start=True, stop=True)
                                    ins.then_inc(s4.h, 1)
                                    t.wait_ge(s2.h, bAa + 6 * i + 2)
                                    ins = None
                                    for c in range(2):
                                        ins = t.matmul(ps_del[:, c * 512:(c + 1) * 512],
                                                       indcol(ident, i),
                                                       dch[:, 1024 + c * 512:
                                                           1024 + (c + 1) * 512],
                                                       start=True, stop=True)
                                    ins.then_inc(s4.h, 1)
                                    t.wait_ge(s2.h, bAa + 6 * i + 3)
                                    ins = None
                                    for c in range(2):
                                        ins = t.matmul(ps_du[:, c * 512:(c + 1) * 512],
                                                       indcol(ident_bf, i),
                                                       uch[:, 1024 + c * 512:
                                                           1024 + (c + 1) * 512],
                                                       start=True, stop=True)
                                    ins.then_inc(s4.h, 1)
                                ymm(126)
                                ymm(127)
                            p.sec("tensor", f_t)
                            s4.n += 512
                            s8.n += 128

                            def f_s(s, dc=dc):
                                def yev(j):
                                    s.wait_ge(s8.h, bTy + j + 1)
                                    if j >= 1:
                                        s.wait_ge(s5.h, bStg + 16 * j)
                                    s.activation(ystage[0][0:1, :], ps_y[0:1, :],
                                                 AF.Copy).then_inc(s6.h, 1)

                                for i in range(128):
                                    d = dc * 128 + i
                                    s.wait_ge(s4.h, bTb + 4 * i + 1)
                                    if i >= 2:
                                        s.wait_ge(s3.h, bVx + 6 * (i - 2) + 6)
                                    for nh in range(NSC):
                                        s.activation(
                                            a_bufs[i % 2][:, nh * T:nh * T + 1024],
                                            ps_del[:, :], AF.Exp,
                                            scale=AT[:, nh * DI + d:nh * DI + d + 1]
                                            ).then_inc(s2.h, 1)
                                    s.wait_ge(s4.h, bTb + 4 * i + 2)
                                    if i >= 2:
                                        s.wait_ge(s3.h, bVx + 6 * (i - 2) + 4)
                                    s.activation(dubc[i % 2][:, 0:1024], ps_du[:, :],
                                                 AF.Identity).then_inc(s2.h, 1)
                                    s.wait_ge(s4.h, bTb + 4 * i + 3)
                                    for nh in range(NSC):
                                        s.activation(
                                            a_bufs[i % 2][:, nh * T + 1024:
                                                          nh * T + 2048],
                                            ps_del[:, :], AF.Exp,
                                            scale=AT[:, nh * DI + d:nh * DI + d + 1]
                                            ).then_inc(s2.h, 1)
                                    s.wait_ge(s4.h, bTb + 4 * i + 4)
                                    s.activation(dubc[i % 2][:, 1024:2048],
                                                 ps_du[:, :], AF.Identity
                                                 ).then_inc(s2.h, 1)
                                    if i >= 2:
                                        yev(i - 2)
                                yev(126)
                                yev(127)
                            p.sec("scalar", f_s)
                            s2.n += 6 * 128
                            s6.n += 128

                            def f_v(v, dc=dc):
                                for i in range(128):
                                    for hh in range(2):
                                        v.wait_ge(s2.h, bAa + 6 * i + 3 * hh + 3)
                                        for nh in range(NSC):
                                            v.tensor_mul(
                                                x_t[:, nh * T + hh * 1024:
                                                    nh * T + (hh + 1) * 1024],
                                                BTt[:, nh * T + hh * 1024:
                                                    nh * T + (hh + 1) * 1024],
                                                dubc[i % 2][:, hh * 1024:
                                                            (hh + 1) * 1024]
                                                ).then_inc(s3.h, 1)
                                    v.wait_ge(s3.h, bVx + 6 * i + 4)
                                    v.wait_ge(s2.h, bAa + 6 * i + 5)
                                    for nh in range(NSC):
                                        v.tensor_tensor_scan(
                                            h_t[:, nh * T:(nh + 1) * T],
                                            a_bufs[i % 2][:, nh * T:(nh + 1) * T],
                                            x_t[:, nh * T:(nh + 1) * T],
                                            0.0, AL.mult, AL.add).then_inc(s3.h, 1)
                                    if i >= 1:
                                        v.wait_ge(s8.h, bTy + i)
                                    for nh in range(NSC):
                                        v.tensor_mul(
                                            g_t[:, nh * T:(nh + 1) * T],
                                            h_t[:, nh * T:(nh + 1) * T],
                                            CTt[:, nh * T:(nh + 1) * T]
                                            ).then_inc(s7.h, 1)
                            p.sec("vector", f_v)
                            s3.n += 6 * 128
                            s7.n += 256

                            def f_gp(gp, dc=dc):
                                for j in range(128):
                                    gp.wait_ge(s6.h, bAe + j + 1)
                                    gp.dma_start(
                                        out=yscan[j:j + 1, dc * T:(dc + 1) * T],
                                        in_=ystage[0][0:1, :]).then_inc(s5.h, 16)
                                gp.wait_ge(s5.h, bStg + 16 * 128)
                            p.sec("gpsimd", f_gp)
                            s5.n += 16 * 128
            for v_ in (dchunks + uchunks + dubc + a_bufs +
                       (x_t, h_t, g_t) + ystage + tlns + (trl,)):
                (v_.arena).release(v_)
            ab.release(BTt)
            ab.release(CTt)
            ab.release(W_delta_bf)

            MILESTONES.append((pidx[0], f"p4_{tb}"))
            gated = ab.alloc(DIC * T)
            with phase() as p:
                def f_v(v):
                    for dc in range(DIC):
                        vdep(v, v.scalar_tensor_tensor(
                            gated[:, dc * T:(dc + 1) * T],
                            u_bf[:, dc * T:(dc + 1) * T],
                            dvec_c[:, dc:dc + 1],
                            yscan[:, dc * T:(dc + 1) * T], AL.mult, AL.add))
                        v.tensor_mul(gated[:, dc * T:(dc + 1) * T],
                                     gated[:, dc * T:(dc + 1) * T],
                                     siluR[:, dc * T:(dc + 1) * T])
                p.sec("vector", f_v)
            ab.release(u_bf)
            ab.release(siluR)
            ab.release(yscan)

            x_mid = af.alloc(DMC * T)
            W_down_bf = ab.alloc(DIC * DM)
            with phase() as p:
                def f_gp(gp):
                    gp.dma_start(out=W_down_bf[:, :],
                                 in_=gi["W_down"].rearrange("(c p) o -> p c o",
                                                            p=128)
                                 ).then_inc(s2.h, 16)
                    gp.wait_ge(s2.h, s2.n + 16)
                p.sec("gpsimd", f_gp)
                s2.n += 16
            emit_tokenwise(b, "resid", x_mid, f"p4{tb}")
            dnsc = (af.alloc(1024), af.alloc(1024))
            with contextlib.ExitStack() as sB2:
                psd = (psum(f"psd0_{tb}", [128, 1024], F32, sB2),
                       psum(f"psd1_{tb}", [128, 1024], F32, sB2))
                with phase() as p:
                    b3, b4, b6 = s3.n, s4.n, s6.n

                    def f_t(t):
                        for do in range(DMC):
                            for th in range(2):
                                idx = do * 2 + th
                                if idx >= 2:
                                    t.wait_ge(s6.h, b6 + idx - 1)
                                ins = None
                                for c2 in range(2):
                                    for kc in range(DIC):
                                        ins = t.matmul(
                                            psd[idx % 2][:, c2 * 512:(c2 + 1) * 512],
                                            W_down_bf[:, kc * DM + do * 128:
                                                      kc * DM + (do + 1) * 128],
                                            gated[:, kc * T + th * 1024 + c2 * 512:
                                                  kc * T + th * 1024 + (c2 + 1) * 512],
                                            start=(kc == 0), stop=(kc == DIC - 1))
                                ins.then_inc(s4.h, 1)
                    p.sec("tensor", f_t)
                    s4.n += 16

                    def f_s(s):
                        for do in range(DMC):
                            for th in range(2):
                                idx = do * 2 + th
                                s.wait_ge(s4.h, b4 + idx + 1)
                                if idx >= 2:
                                    s.wait_ge(s3.h, b3 + idx - 1)
                                s.activation(dnsc[idx % 2][:, :],
                                             psd[idx % 2][:, :], AF.Identity,
                                             bias=bdown_c[:, do:do + 1]
                                             ).then_inc(s6.h, 1)
                    p.sec("scalar", f_s)
                    s6.n += 16

                    def f_v(v):
                        for do in range(DMC):
                            for th in range(2):
                                idx = do * 2 + th
                                v.wait_ge(s6.h, b6 + idx + 1)
                                sl = slice(do * T + th * 1024,
                                           do * T + (th + 1) * 1024)
                                v.tensor_add(x_mid[:, sl], x_mid[:, sl],
                                             dnsc[idx % 2][:, :]).then_inc(s3.h, 1)
                    p.sec("vector", f_v)
                    s3.n += 16
            af.release(dnsc[0])
            af.release(dnsc[1])
            ab.release(gated)
            ab.release(W_down_bf)

            MILESTONES.append((pidx[0], f"p5_{tb}"))
            tokT = ab.alloc(DMC * T)
            hT = ab.alloc(32 * 1024)
            w1bufs = (ab.alloc(8 * 256), ab.alloc(8 * 256))
            w2bufs = (ab.alloc(16 * 128), ab.alloc(16 * 128))
            mu3 = af.alloc(T)
            r13 = af.alloc(T)
            sq3 = af.alloc(T)
            with contextlib.ExitStack() as sLs:
                ps_st = psum(f"ps_st3_{tb}", [128, T], F32, sLs)
                with phase() as p:
                    def f_t(t):
                        for dc in range(DMC):
                            for c in range(4):
                                t.matmul(ps_st[:, c * 512:(c + 1) * 512],
                                         ones_f[:, :],
                                         x_mid[:, dc * T + c * 512:
                                               dc * T + (c + 1) * 512],
                                         start=(dc == 0), stop=(dc == DMC - 1))
                    p.sec("tensor", f_t)
                with phase() as p:
                    p.sec("vector", lambda v:
                          v.tensor_scalar_mul(mu3[:, :], ps_st[:, :], 1.0 / DM))
                with phase() as p:
                    b3, b4 = s3.n, s4.n

                    def f_s(s):
                        for dc in range(DMC):
                            if dc >= 1:
                                s.wait_ge(s4.h, b4 + dc)
                            s.activation(sq3[:, :], x_mid[:, dc * T:(dc + 1) * T],
                                         AF.Square).then_inc(s3.h, 1)
                    p.sec("scalar", f_s)
                    s3.n += DMC

                    def f_t(t):
                        for dc in range(DMC):
                            t.wait_ge(s3.h, b3 + dc + 1)
                            ins = None
                            for c in range(4):
                                ins = t.matmul(ps_st[:, c * 512:(c + 1) * 512],
                                               ones_f[:, :],
                                               sq3[:, c * 512:(c + 1) * 512],
                                               start=(dc == 0), stop=(dc == DMC - 1))
                            ins.then_inc(s4.h, 1)
                    p.sec("tensor", f_t)
                    s4.n += DMC
                with phase() as p:
                    def f_v(v):
                        vdep(v, v.tensor_scalar_mul(r13[:, :], ps_st[:, :], 1.0 / DM))
                        vdep(v, v.tensor_mul(sq3[:, :], mu3[:, :], mu3[:, :]))
                        vdep(v, v.tensor_sub(r13[:, :], r13[:, :], sq3[:, :]))
                        v.tensor_scalar_add(r13[:, :], r13[:, :], EPS)
                    p.sec("vector", f_v)
                with phase() as p:
                    def f_s(s):
                        adep(s, s.activation(sq3[:, :], r13[:, :], AF.Ln))
                        s.activation(r13[:, :], sq3[:, :], AF.Exp, scale=-0.5)
                    p.sec("scalar", f_s)
                with phase() as p:
                    def f_v(v):
                        for dc in range(DMC):
                            vdep(v, v.tensor_sub(sq3[:, :],
                                                 x_mid[:, dc * T:(dc + 1) * T],
                                                 mu3[:, :]))
                            v.tensor_mul(tokT[:, dc * T:(dc + 1) * T],
                                         sq3[:, :], r13[:, :])
                    p.sec("vector", f_v)
            af.release(r13)
            af.release(sq3)
            mask_tm = af.alloc(16 * 8)
            mask_fm = af.alloc(T, p=8)

            with contextlib.ExitStack() as sL2:
                ps_l = psum(f"ps_l_{tb}", [128, 8], F32, sL2)
                ps_mf = psum(f"ps_mf_{tb}", [8, T], F32, sL2)
                with phase() as p:
                    b3, b4 = s3.n, s4.n

                    def f_t(t):
                        for tc in range(16):
                            if tc >= 1:
                                t.wait_ge(s3.h, b3 + tc)
                            for dc in range(DMC):
                                t.matmul(ps_l[:, 0:8],
                                         x_mid[:, dc * T + tc * 128:
                                               dc * T + (tc + 1) * 128],
                                         gwT[:, dc * 8:(dc + 1) * 8],
                                         start=(dc == 0), stop=False)
                            t.matmul(ps_l[:, 0:8],
                                     mu3[0:1, tc * 128:(tc + 1) * 128],
                                     ngs_row[0:1, 0:8], start=False, stop=True
                                     ).then_inc(s4.h, 1)
                    p.sec("tensor", f_t)
                    s4.n += 16

                    def f_v(v):
                        for tc in range(16):
                            v.wait_ge(s4.h, b4 + tc + 1)
                            vdep(v, v.reduce_max(rmax[:, 0:1], ps_l[:, 0:8],
                                                 axis=AX.X))
                            v.tensor_single_scalar(mask_tm[:, tc * 8:(tc + 1) * 8],
                                                   ps_l[:, 0:8], rmax[:, 0:1],
                                                   AL.is_ge).then_inc(s3.h, 1)
                    p.sec("vector", f_v)
                    s3.n += 16
                with phase() as p:
                    def f_t(t):
                        for tc in range(16):
                            t.transpose(ps_mf[0:8, tc * 128:(tc + 1) * 128],
                                        mask_tm[:, tc * 8:(tc + 1) * 8],
                                        ident[:, :])
                    p.sec("tensor", f_t)
                with phase() as p:
                    p.sec("vector", lambda v:
                          v.tensor_copy(mask_fm[0:8, :], ps_mf[0:8, :]))
            af.release(mu3)
            mask_bc = af.alloc(T)
            tmpo = af.alloc(1024)

            MILESTONES.append((pidx[0], f"p6_{tb}"))
            with contextlib.ExitStack() as sF:
                ps_h = (psum(f"psh0_{tb}", [128, 1024], F32, sF),
                        psum(f"psh1_{tb}", [128, 1024], F32, sF))
                ps_o = (psum(f"pso0_{tb}", [128, 1024], F32, sF),
                        psum(f"pso1_{tb}", [128, 1024], F32, sF))

                for e in range(NEXP):
                    with phase() as p:
                        def f_sy(sy, e=e):
                            base = s1.n
                            sy.dma_start(out=mask_bc[0:1, :],
                                         in_=mask_fm[e:e + 1, :]).then_inc(s1.h, 16)
                            k, nd = 1, 1
                            while k < 128:
                                sy.wait_ge(s1.h, base + 16 * nd)
                                sy.dma_start(out=mask_bc[k:2 * k, 0:T],
                                             in_=mask_bc[0:k, 0:T]).then_inc(s1.h, 16)
                                k *= 2
                                nd += 1
                            sy.wait_ge(s1.h, base + 16 * 8)
                        p.sec("sync", f_sy)
                        s1.n += 16 * 8

                    with phase() as p:
                        bW1, bW2 = s1.n, s2.n
                        bTh, bAg = s5.n, s6.n
                        bTo, bVo = s7.n, s8.n

                        def f_gp(gp, e=e):
                            for ts in range(2):
                                for fq in range(16):
                                    j = ts * 16 + fq
                                    if j >= 2:
                                        gp.wait_ge(s5.h, bTh + 2 * (j - 1))
                                    gp.dma_start(
                                        out=w1bufs[j % 2][:, :],
                                        in_=gi["w1"][e, :, fq * 256:(fq + 1) * 256]
                                        .rearrange("(c p) f -> p c f", p=128)
                                        ).then_inc(s1.h, 16)
                                for dh in range(16):
                                    j = ts * 16 + dh
                                    doc, hf = divmod(dh, 2)
                                    if j >= 2:
                                        gp.wait_ge(s7.h, bTo + (j - 1))
                                    gp.dma_start(
                                        out=w2bufs[j % 2][:, :],
                                        in_=gi["w2"][e, hf * 2048:(hf + 1) * 2048,
                                                     doc * 128:(doc + 1) * 128]
                                        .rearrange("(c p) o -> p c o", p=128)
                                        ).then_inc(s2.h, 16)
                        p.sec("gpsimd", f_gp)
                        s1.n += 16 * 32
                        s2.n += 16 * 32

                        def f_t(t, e=e):
                            for ts in range(2):
                                for fq in range(16):
                                    j = ts * 16 + fq
                                    t.wait_ge(s1.h, bW1 + 16 * (j + 1))
                                    for sub in range(2):
                                        fc = fq * 2 + sub
                                        jj = j * 2 + sub
                                        if jj >= 2:
                                            t.wait_ge(s6.h, bAg + (jj - 1))
                                        ins = None
                                        for c2 in range(2):
                                            for kc in range(DMC):
                                                ins = t.matmul(
                                                    ps_h[fc % 2][:, c2 * 512:(c2 + 1) * 512],
                                                    w1bufs[j % 2][:, kc * 256 + sub * 128:
                                                                  kc * 256 + (sub + 1) * 128],
                                                    tokT[:, kc * T + ts * 1024 + c2 * 512:
                                                         kc * T + ts * 1024 + (c2 + 1) * 512],
                                                    start=(kc == 0), stop=(kc == DMC - 1))
                                        ins.then_inc(s5.h, 1)
                                for dh in range(16):
                                    j = ts * 16 + dh
                                    doc, hf = divmod(dh, 2)
                                    t.wait_ge(s2.h, bW2 + 16 * (j + 1))
                                    t.wait_ge(s6.h, bAg + 32 * (ts + 1))
                                    if hf == 0 and ts * 8 + doc >= 2:
                                        t.wait_ge(s8.h, bVo + (ts * 8 + doc - 1))
                                    ins = None
                                    for c2 in range(2):
                                        for fc in range(16):
                                            gfc = hf * 16 + fc
                                            ins = t.matmul(
                                                ps_o[doc % 2][:, c2 * 512:(c2 + 1) * 512],
                                                w2bufs[j % 2][:, fc * 128:(fc + 1) * 128],
                                                hT[:, gfc * 1024 + c2 * 512:
                                                    gfc * 1024 + (c2 + 1) * 512],
                                                start=(gfc == 0), stop=(gfc == 31))
                                    ins.then_inc(s7.h, 1)
                        p.sec("tensor", f_t)
                        s5.n += 64
                        s7.n += 32

                        def f_s(s, e=e):
                            for ts in range(2):
                                for fc in range(32):
                                    j = ts * 32 + fc
                                    s.wait_ge(s5.h, bTh + j + 1)
                                    s.activation(hT[:, fc * 1024:(fc + 1) * 1024],
                                                 ps_h[fc % 2][:, :], AF.Gelu
                                                 ).then_inc(s6.h, 1)
                        p.sec("scalar", f_s)
                        s6.n += 64

                        def f_v(v, e=e):
                            for ts in range(2):
                                for doc in range(8):
                                    v.wait_ge(s7.h, bTo + ts * 16 + doc * 2 + 2)
                                    vdep(v, v.tensor_mul(
                                        tmpo[:, :], ps_o[doc % 2][:, :],
                                        mask_bc[:, ts * 1024:(ts + 1) * 1024]))
                                    v.tensor_add(
                                        x_mid[:, doc * T + ts * 1024:
                                              doc * T + (ts + 1) * 1024],
                                        x_mid[:, doc * T + ts * 1024:
                                              doc * T + (ts + 1) * 1024],
                                        tmpo[:, :]).then_inc(s8.h, 1)
                        p.sec("vector", f_v)
                        s8.n += 16
            for v_ in (tokT, hT, w1bufs[0], w1bufs[1], w2bufs[0], w2bufs[1]):
                ab.release(v_)
            for v_ in (mask_tm, mask_bc, mask_fm, tmpo):
                af.release(v_)

            MILESTONES.append((pidx[0], f"p7_{tb}"))
            otm = (af.alloc(DM), af.alloc(DM))
            with contextlib.ExitStack() as sO:
                ps_ot = psum(f"ps_ot_{tb}", [128, DM], F32, sO)
                with phase() as p:
                    b3, b4 = s3.n, s4.n
                    bSt = sOut.n

                    def f_t(t, b=b):
                        for i in range(16):
                            if i >= 1:
                                t.wait_ge(s3.h, b3 + i)
                            ins = None
                            for dc in range(DMC):
                                ins = t.transpose(ps_ot[:, dc * 128:(dc + 1) * 128],
                                                  x_mid[:, dc * T + i * 128:
                                                        dc * T + (i + 1) * 128],
                                                  ident[:, :])
                            ins.then_inc(s4.h, 1)
                    p.sec("tensor", f_t)
                    s4.n += 16

                    def f_v(v, b=b):
                        for i in range(16):
                            v.wait_ge(s4.h, b4 + i + 1)
                            if i >= 2:
                                v.wait_ge(sOut.h, bSt + 16 * (i - 1))
                            v.tensor_copy(otm[i % 2][:, :], ps_ot[:, :]
                                          ).then_inc(s3.h, 1)
                    p.sec("vector", f_v)
                    s3.n += 16

                    def f_sy(sy, b=b):
                        for i in range(16):
                            sy.wait_ge(s3.h, b3 + i + 1)
                            sy.dma_start(out=y_out[b, i * 128:(i + 1) * 128, :],
                                         in_=otm[i % 2][:, :]).then_inc(sOut.h, 16)
                        sy.wait_ge(sOut.h, bSt + 16 * 16)
                    p.sec("sync", f_sy)
                    sOut.n += 16 * 16
            af.release(otm[0])
            af.release(otm[1])
            af.release(x_mid)

        with phase() as p:
            p.sec("sync", lambda sy: sy.wait_ge(sOut.h, sOut.n))
    return nc


def kernel(**inputs):
    x = np.ascontiguousarray(inputs["L0_attn_x"])
    nm = {
        "ln_gamma": "L0_attn_ln_gamma", "ln_beta": "L0_attn_ln_beta",
        "W_up": "L0_attn_W_up", "b_up": "L0_attn_b_up",
        "conv_w": "L0_attn_conv_w", "conv_b": "L0_attn_conv_b",
        "W_delta": "L0_attn_W_delta", "b_delta": "L0_attn_b_delta",
        "W_B": "L0_attn_W_B", "W_C": "L0_attn_W_C",
        "A_log": "L0_attn_A_log", "Dvec": "L0_attn_D",
        "W_down": "L0_attn_W_down", "b_down": "L0_attn_b_down",
        "gate_w": "L0_ffn_gate_w", "w1": "L0_ffn_w1", "w2": "L0_ffn_w2",
    }
    weights = {k: np.ascontiguousarray(inputs[v]) for k, v in nm.items()}
    nc = build()
    in_maps = []
    for c in range(8):
        m = dict(weights)
        m["x_in"] = x[2 * c:2 * c + 2]
        in_maps.append(m)
    res = run_bass_kernel_spmd(nc, in_maps, core_ids=list(range(8)))
    return np.concatenate([r["y_out"] for r in res.results], axis=0)


# version 09 (final)
# speedup vs baseline: 1.9930x; 1.0083x over previous
import contextlib
import numpy as np

import concourse.bass as bass
import concourse.mybir as mybir
from concourse.bass_utils import run_bass_kernel_spmd

F32 = mybir.dt.float32
BF16 = mybir.dt.bfloat16
AL = mybir.AluOpType
AF = mybir.ActivationFunctionType
AX = mybir.AxisListType

NB = 2
T = 2048
DM = 1024
DI = 512
NS = 256
NEXP = 8
FF = 4096
EPS = 1e-5
DMC = DM // 128
DIC = DI // 128
NSC = NS // 128

ENGS = ("sync", "vector", "scalar", "tensor", "gpsimd")
PHASE_LIMIT = None
MILESTONES = []


class Sem:
    __slots__ = ("h", "n")

    def __init__(self, nc, es, name):
        self.h = es.enter_context(nc.semaphore(name))
        self.n = 0


class Phase:
    def __init__(self, nc, bar, idx):
        self.bar = bar
        self.idx = idx
        self.blk = nc.Block()
        self.done = set()

    def __enter__(self):
        self.blk.__enter__()
        return self

    def sec(self, eng_name, fn):
        assert eng_name not in self.done, eng_name
        self.done.add(eng_name)
        if PHASE_LIMIT is not None and self.idx >= PHASE_LIMIT:
            return
        target = 5 * self.idx
        dec = getattr(self.blk, eng_name)

        def wrapped(eng):
            eng.wait_ge(self.bar.h, target)
            fn(eng)
            eng.maybe_drain_then_inc((self.bar.h, 1))

        dec(wrapped)
        self.bar.n += 1

    def __exit__(self, *a):
        for e in ENGS:
            if e not in self.done:
                self.sec(e, lambda _eng: None)
        return self.blk.__exit__(*a)


class View:
    __slots__ = ("arena", "off", "n", "p")

    def __init__(self, arena, off, n, p=128):
        self.arena = arena
        self.off = off
        self.n = n
        self.p = p

    def __getitem__(self, key):
        ps, fs = key
        start = fs.start if fs.start is not None else 0
        stop = fs.stop if fs.stop is not None else self.n
        assert 0 <= start <= stop <= self.n, (start, stop, self.n)
        return self.arena.t[ps, self.off + start: self.off + stop]

    def mkap(self, extra_off, dims):
        base = self.arena.t.ap()
        return bass.AP(base.tensor, base.offset + self.off + extra_off, dims)


class Arena:
    def __init__(self, nc, es, name, dtype, nelem):
        self.t = es.enter_context(nc.sbuf_tensor(name, [128, nelem], dtype))
        self.cap = nelem
        self.free = [(0, nelem)]
        self.name = name

    def alloc(self, nelem, p=128):
        nelem_a = (nelem + 127) & ~127
        for i, (off, sz) in enumerate(self.free):
            if sz >= nelem_a:
                if sz == nelem_a:
                    self.free.pop(i)
                else:
                    self.free[i] = (off + nelem_a, sz - nelem_a)
                return View(self, off, nelem, p)
        raise MemoryError(f"arena {self.name}: need {nelem_a}, free={self.free}")

    def release(self, view):
        nelem_a = (view.n + 127) & ~127
        self.free.append((view.off, nelem_a))
        self.free.sort()
        merged = []
        for off, sz in self.free:
            if merged and merged[-1][0] + merged[-1][1] == off:
                merged[-1] = (merged[-1][0], merged[-1][1] + sz)
            else:
                merged.append((off, sz))
        self.free = [tuple(x) for x in merged]


def build():
    nc = bass.Bass("TRN2", target_bir_lowering=False)

    x_in = nc.dram_tensor("x_in", [NB, T, DM], F32, kind="ExternalInput").ap()
    gi = {}
    for name, shape in [
        ("ln_gamma", [DM]), ("ln_beta", [DM]), ("W_up", [DM, DM]),
        ("b_up", [DM]), ("conv_w", [DI, 1, 3]), ("conv_b", [DI]),
        ("W_delta", [DI, DI]), ("b_delta", [DI]), ("W_B", [DI, NS]),
        ("W_C", [DI, NS]), ("A_log", [DI, NS]), ("Dvec", [DI]),
        ("W_down", [DI, DM]), ("b_down", [DM]), ("gate_w", [NEXP, DM]),
        ("w1", [NEXP, DM, FF]), ("w2", [NEXP, FF, DM]),
    ]:
        gi[name] = nc.dram_tensor(name, shape, F32, kind="ExternalInput").ap()
    y_out = nc.dram_tensor("y_out", [NB, T, DM], F32, kind="ExternalOutput").ap()

    es = contextlib.ExitStack()
    with es:
        def sb(name, shape, dt, stack=None):
            return (stack or es).enter_context(nc.sbuf_tensor(name, shape, dt))

        def psum(name, shape, dt, stack=None):
            return (stack or es).enter_context(nc.psum_tensor(name, shape, dt))

        ident = sb("ident", [128, 128], F32)
        ident_bf = sb("ident_bf", [128, 128], BF16)
        ones_f = sb("ones_f", [128, 128], F32)
        ones_bf = sb("ones_bf", [128, 1], BF16)
        cw_cols = sb("cw_cols", [128, DIC * 3], F32)
        bup_c = sb("bup_c", [128, DMC], F32)
        convb_c = sb("convb_c", [128, DIC], F32)
        bdelta_c = sb("bdelta_c", [128, DIC], F32)
        dvec_c = sb("dvec_c", [128, DIC], F32)
        bdown_c = sb("bdown_c", [128, DMC], F32)
        AT = sb("AT", [128, NSC * DI], F32)
        gwT = sb("gwT", [128, DMC * 8], F32)
        ngs_row = sb("ngs_row", [1, 8], F32)
        statc = sb("statc", [128, 16], F32)
        rmax = sb("rmax", [128, 1], F32)

        af = Arena(nc, es, "af", F32, 22528)
        ab = Arena(nc, es, "ab", BF16, 57344)

        bar = Sem(nc, es, "bar")
        s1 = Sem(nc, es, "s1")
        s2 = Sem(nc, es, "s2")
        s3 = Sem(nc, es, "s3")
        s4 = Sem(nc, es, "s4")
        s5 = Sem(nc, es, "s5")
        s6 = Sem(nc, es, "s6")
        s7 = Sem(nc, es, "s7")
        s8 = Sem(nc, es, "s8")
        sOut = Sem(nc, es, "sOut")
        sVd = Sem(nc, es, "sVd")
        sAd = Sem(nc, es, "sAd")

        def vdep(v, ins):
            sVd.n += 1
            ins.then_inc(sVd.h, 1)
            v.wait_ge(sVd.h, sVd.n)

        def adep(s, ins):
            sAd.n += 1
            ins.then_inc(sAd.h, 1)
            s.wait_ge(sAd.h, sAd.n)

        pidx = [0]

        def phase():
            p = Phase(nc, bar, pidx[0])
            pidx[0] += 1
            return p

        ini = contextlib.ExitStack()
        iot = sb("iot", [128, 128], F32, ini)
        rowbuf = af.alloc(1024, p=1)
        gw_rows = af.alloc(DM, p=8)
        A_t = af.alloc(DIC * NS)

        with phase() as p:
            def f_gp(gp):
                gp.iota(iot[:, 0:128], [[1, 128]], channel_multiplier=-1,
                        allow_small_or_imprecise_dtypes=True).then_inc(s3.h, 1)
                s3.n += 1
                gp.dma_start(out=A_t[:, :],
                             in_=gi["A_log"].rearrange("(c p) n -> p c n", p=128)
                             ).then_inc(s2.h, 16)
                s2.n += 16
                gp.dma_start(out=cw_cols[:, :],
                             in_=gi["conv_w"].rearrange("(c p) u k -> p c u k",
                                                        p=128)
                             ).then_inc(s2.h, 16)
                s2.n += 16
            p.sec("gpsimd", f_gp)

            def f_v(v):
                v.wait_ge(s3.h, 1)
                vdep(v, v.tensor_single_scalar(ident[:, :], iot[:, 0:128],
                                               0.0, AL.is_equal))
                v.tensor_copy(ident_bf[:, :], ident[:, :])
                v.memset(ones_f[:, :], 1.0)
                v.memset(ones_bf[:, :], 1.0)
            p.sec("vector", f_v)

        vec_loads = [
            ("b_up", bup_c, DMC), ("conv_b", convb_c, DIC),
            ("b_delta", bdelta_c, DIC), ("Dvec", dvec_c, DIC),
            ("b_down", bdown_c, DMC),
        ]
        ps_col = psum("ps_col", [128, 16], F32, ini)
        for (srcname, dst, nch) in vec_loads:
            src = gi[srcname]
            with phase() as p:
                def f_sy(sy, src=src):
                    sy.dma_start(out=rowbuf[0:1, 0:src.shape[0]],
                                 in_=src.rearrange("(a d) -> a d", a=1)
                                 ).then_inc(s1.h, 16)
                    sy.wait_ge(s1.h, s1.n + 16)
                p.sec("sync", f_sy)
                s1.n += 16
            with phase() as p:
                def f_t(t, nch=nch):
                    for c in range(nch):
                        t.transpose(ps_col[:, c:c + 1],
                                    rowbuf[0:1, c * 128:(c + 1) * 128],
                                    ident[0:1, 0:1])
                p.sec("tensor", f_t)
            with phase() as p:
                p.sec("vector", lambda v, dst=dst, nch=nch:
                      v.tensor_copy(dst[:, 0:nch], ps_col[:, 0:nch]))

        ps_ngs = psum("ps_ngs", [1, 8], F32, ini)
        with phase() as p:
            def f_sy(sy):
                sy.dma_start(out=gw_rows[0:8, :], in_=gi["gate_w"][:, :]
                             ).then_inc(s1.h, 16)
                sy.wait_ge(s1.h, s1.n + 16)
            p.sec("sync", f_sy)
            s1.n += 16
        for dc in range(DMC):
            with phase() as p:
                p.sec("tensor", lambda t, dc=dc:
                      t.transpose(ps_col[:, 0:8],
                                  gw_rows[0:8, dc * 128:(dc + 1) * 128],
                                  ident[0:8, 0:8]))
            with phase() as p:
                p.sec("vector", lambda v, dc=dc:
                      v.tensor_copy(gwT[:, dc * 8:(dc + 1) * 8], ps_col[:, 0:8]))
        with phase() as p:
            def f_t(t):
                for dc in range(DMC):
                    t.matmul(ps_ngs[0:1, 0:8], ones_f[:, 0:1],
                             gwT[:, dc * 8:(dc + 1) * 8],
                             start=(dc == 0), stop=(dc == DMC - 1))
            p.sec("tensor", f_t)
        with phase() as p:
            p.sec("vector", lambda v:
                  v.tensor_scalar_mul(ngs_row[0:1, :], ps_ngs[0:1, :], -1.0))

        with phase() as p:
            def f_s(s):
                s.wait_ge(s2.h, s2.n)
                s.activation(A_t[:, :], A_t[:, :], AF.Exp)
            p.sec("scalar", f_s)
        with phase() as p:
            p.sec("vector", lambda v:
                  v.tensor_scalar_mul(A_t[:, :], A_t[:, :], -1.0))
        ps_at = psum("ps_at", [128, 128], F32, ini)
        for i in range(DIC * NSC):
            dc, h = divmod(i, NSC)
            with phase() as p:
                p.sec("tensor", lambda t, dc=dc, h=h:
                      t.transpose(ps_at[:, :],
                                  A_t[:, dc * NS + h * 128: dc * NS + (h + 1) * 128],
                                  ident[:, :]))
            with phase() as p:
                p.sec("vector", lambda v, dc=dc, h=h:
                      v.tensor_copy(AT[:, h * DI + dc * 128: h * DI + (dc + 1) * 128],
                                    ps_at[:, :]))
        ini.close()
        af.release(rowbuf)
        af.release(gw_rows)
        af.release(A_t)

        def emit_tokenwise(b, mode, dstT, tag, gam_bc=None, bet_bc=None):
            st = contextlib.ExitStack()
            xtm = (af.alloc(DM), af.alloc(DM))
            sqtm = (af.alloc(DM), af.alloc(DM))
            ytmp = sqtm
            if mode == "ln2":
                otm = (ab.alloc(DM), ab.alloc(DM))
                ps_tp = psum(f"pstp_{tag}", [128, DM], BF16, st)
            else:
                otm = ytmp
                ps_tp = psum(f"pstp_{tag}", [128, DM], F32, st)
            with st:
                with phase() as p:
                    b1, b3, b4, b6, b8 = s1.n, s3.n, s4.n, s6.n, s8.n

                    def f_sy(sy):
                        for i in range(16):
                            if i >= 2:
                                sy.wait_ge(s3.h, b3 + 3 * (i - 2) + 2)
                            sy.dma_start(out=xtm[i % 2][:, :],
                                         in_=x_in[b, i * 128:(i + 1) * 128, :]
                                         ).then_inc(s1.h, 16)
                    p.sec("sync", f_sy)
                    s1.n += 16 * 16

                    def f_s(s):
                        for i in range(16):
                            c0 = (i % 2) * 8
                            s.wait_ge(s1.h, b1 + 16 * (i + 1))
                            if i >= 2:
                                s.wait_ge(s3.h, b3 + 3 * (i - 2) + 2)
                                s.wait_ge(s8.h, b8 + (i - 2) + 1)
                            s.activation(sqtm[i % 2][:, :], xtm[i % 2][:, :],
                                         AF.Square).then_inc(s4.h, 1)
                            s.wait_ge(s3.h, b3 + 3 * i + 1)
                            adep(s, s.activation(statc[:, c0 + 2:c0 + 3],
                                                 statc[:, c0 + 6:c0 + 7], AF.Ln))
                            s.activation(statc[:, c0 + 7:c0 + 8],
                                         statc[:, c0 + 2:c0 + 3], AF.Exp,
                                         scale=-0.5).then_inc(s6.h, 1)
                    p.sec("scalar", f_s)
                    s4.n += 16
                    s6.n += 16

                    def f_v(v):
                        for i in range(16):
                            c0 = (i % 2) * 8
                            raws = statc[:, c0 + 0:c0 + 1]
                            mu = statc[:, c0 + 1:c0 + 2]
                            rawq = statc[:, c0 + 2:c0 + 3]
                            ex2 = statc[:, c0 + 3:c0 + 4]
                            musq = statc[:, c0 + 4:c0 + 5]
                            var = statc[:, c0 + 5:c0 + 6]
                            arg = statc[:, c0 + 6:c0 + 7]
                            sc = statc[:, c0 + 7:c0 + 8]
                            v.wait_ge(s1.h, b1 + 16 * (i + 1))
                            vdep(v, v.reduce_sum(raws, xtm[i % 2][:, :], axis=AX.X))
                            vdep(v, v.tensor_scalar_mul(mu, raws, 1.0 / DM))
                            v.wait_ge(s4.h, b4 + i + 1)
                            vdep(v, v.reduce_sum(rawq, sqtm[i % 2][:, :], axis=AX.X))
                            vdep(v, v.tensor_scalar_mul(ex2, rawq, 1.0 / DM))
                            vdep(v, v.tensor_mul(musq, mu, mu))
                            vdep(v, v.tensor_sub(var, ex2, musq))
                            if mode == "ln2":
                                ins = v.tensor_scalar(arg, var, 1.0 + EPS,
                                                      EPS * EPS, AL.mult, AL.add)
                            else:
                                ins = v.tensor_scalar_add(arg, var, EPS)
                            ins.then_inc(s3.h, 1)
                            v.wait_ge(s6.h, b6 + i + 1)
                            yt = ytmp[i % 2]
                            vdep(v, v.tensor_scalar(yt[:, :], xtm[i % 2][:, :],
                                                    mu, sc, AL.subtract, AL.mult))
                            if mode == "ln2":
                                vdep(v, v.tensor_mul(yt[:, :], yt[:, :],
                                                     gam_bc[:, 0:DM]))
                                v.tensor_add(otm[i % 2][:, :], yt[:, :],
                                             bet_bc[:, 0:DM]).then_inc(s3.h, 1)
                            else:
                                v.tensor_add(yt[:, :], yt[:, :],
                                             xtm[i % 2][:, :]).then_inc(s3.h, 1)
                            v.wait_ge(s8.h, b8 + i + 1)
                            dst = dstT.mkap(i * 128,
                                            [[dstT.arena.cap, 128], [T, DMC],
                                             [1, 128]])
                            v.tensor_copy(dst, ps_tp[:, :].rearrange(
                                "p (c n) -> p c n", c=DMC)).then_inc(s3.h, 1)
                    p.sec("vector", f_v)
                    s3.n += 3 * 16

                    def f_t(t):
                        idf = ident_bf if mode == "ln2" else ident
                        for i in range(16):
                            t.wait_ge(s3.h, b3 + 3 * i + 2)
                            if i >= 1:
                                t.wait_ge(s3.h, b3 + 3 * (i - 1) + 3)
                            ins = None
                            for dc in range(DMC):
                                ins = t.transpose(ps_tp[:, dc * 128:(dc + 1) * 128],
                                                  otm[i % 2][:, dc * 128:(dc + 1) * 128],
                                                  idf[:, :])
                            ins.then_inc(s8.h, 1)
                    p.sec("tensor", f_t)
                    s8.n += 16
            for v_ in xtm + sqtm:
                af.release(v_)
            if mode == "ln2":
                ab.release(otm[0])
                ab.release(otm[1])

        for b in range(NB):
            tb = f"b{b}"
            MILESTONES.append((pidx[0], f"p1_{tb}"))
            y2T = ab.alloc(DMC * T)
            W_up_bf = ab.alloc(DMC * DM)
            with phase() as p:
                def f_gp(gp):
                    gp.dma_start(out=W_up_bf[:, :],
                                 in_=gi["W_up"].rearrange("(c p) o -> p c o", p=128)
                                 ).then_inc(s2.h, 16)
                    gp.wait_ge(s2.h, s2.n + 16)
                p.sec("gpsimd", f_gp)
                s2.n += 16
            gam_bc = af.alloc(DM)
            bet_bc = af.alloc(DM)
            with phase() as p:
                def f_sy(sy):
                    for dstv, srcname in ((gam_bc, "ln_gamma"), (bet_bc, "ln_beta")):
                        base = s1.n
                        sy.dma_start(out=dstv[0:1, :],
                                     in_=gi[srcname].rearrange("(a d) -> a d", a=1)
                                     ).then_inc(s1.h, 16)
                        k, nd = 1, 1
                        while k < 128:
                            sy.wait_ge(s1.h, base + 16 * nd)
                            sy.dma_start(out=dstv[k:2 * k, 0:DM],
                                         in_=dstv[0:k, 0:DM]).then_inc(s1.h, 16)
                            k *= 2
                            nd += 1
                        sy.wait_ge(s1.h, base + 16 * 8)
                        s1.n += 16 * 8
                p.sec("sync", f_sy)
            emit_tokenwise(b, "ln2", y2T, f"p1{tb}", gam_bc, bet_bc)
            af.release(gam_bc)
            af.release(bet_bc)

            MILESTONES.append((pidx[0], f"p2_{tb}"))
            u_bf = ab.alloc(DIC * T)
            siluR = ab.alloc(DIC * T)
            BTt = ab.alloc(NSC * T)
            CTt = ab.alloc(NSC * T)
            W_delta_bf = ab.alloc(DIC * DI)
            W_B_bf = ab.alloc(DIC * NS)
            W_C_bf = ab.alloc(DIC * NS)
            with phase() as p:
                def f_gp(gp):
                    n0 = s2.n
                    for (dst, src) in [(W_delta_bf, gi["W_delta"]),
                                       (W_B_bf, gi["W_B"]), (W_C_bf, gi["W_C"])]:
                        gp.dma_start(out=dst[:, :],
                                     in_=src.rearrange("(c p) o -> p c o", p=128)
                                     ).then_inc(s2.h, 16)
                    gp.wait_ge(s2.h, n0 + 48)
                p.sec("gpsimd", f_gp)
                s2.n += 48

            Lcs = tuple(af.alloc(T) for _ in range(4))
            cvs = (af.alloc(T), af.alloc(T))
            with contextlib.ExitStack() as sA2:
                psu = (psum(f"psu0_{tb}", [128, 1024], F32, sA2),
                       psum(f"psu1_{tb}", [128, 1024], F32, sA2))

                with phase() as p:
                    b3, b4 = s3.n, s4.n

                    def f_t(t):
                        for do in range(DMC):
                            for th in range(2):
                                idx = do * 2 + th
                                if idx >= 2:
                                    t.wait_ge(s3.h, b3 + idx - 1)
                                ins = None
                                for c2 in range(2):
                                    for kc in range(DMC):
                                        ins = t.matmul(
                                            psu[idx % 2][:, c2 * 512:(c2 + 1) * 512],
                                            W_up_bf[:, kc * DM + do * 128:
                                                    kc * DM + (do + 1) * 128],
                                            y2T[:, kc * T + th * 1024 + c2 * 512:
                                                kc * T + th * 1024 + (c2 + 1) * 512],
                                            start=(kc == 0), stop=(kc == DMC - 1))
                                ins.then_inc(s4.h, 1)
                    p.sec("tensor", f_t)
                    s4.n += 16

                    def f_s(s):
                        for do in range(DMC):
                            for th in range(2):
                                idx = do * 2 + th
                                s.wait_ge(s4.h, b4 + idx + 1)
                                if do < DIC:
                                    s.activation(
                                        Lcs[do][:, th * 1024:(th + 1) * 1024],
                                        psu[idx % 2][:, :], AF.Identity,
                                        bias=bup_c[:, do:do + 1]).then_inc(s3.h, 1)
                                else:
                                    dc = do - DIC
                                    s.activation(
                                        siluR[:, dc * T + th * 1024:
                                              dc * T + (th + 1) * 1024],
                                        psu[idx % 2][:, :], AF.Silu,
                                        bias=bup_c[:, do:do + 1]).then_inc(s3.h, 1)
                    p.sec("scalar", f_s)
                    s3.n += 16
                ab.release(y2T)
                ab.release(W_up_bf)

                with phase() as p:
                    b3, b4 = s3.n, s4.n

                    def f_v(v):
                        for dc in range(DIC):
                            if dc >= 2:
                                v.wait_ge(s4.h, b4 + dc - 1)
                            L = Lcs[dc]
                            cv = cvs[dc % 2]
                            w0 = cw_cols[:, dc * 3 + 0: dc * 3 + 1]
                            w1 = cw_cols[:, dc * 3 + 1: dc * 3 + 2]
                            w2 = cw_cols[:, dc * 3 + 2: dc * 3 + 3]
                            vdep(v, v.memset(cv[:, 0:2], 0.0))
                            vdep(v, v.tensor_scalar_mul(cv[:, 2:T], L[:, 0:T - 2], w0))
                            vdep(v, v.scalar_tensor_tensor(cv[:, 1:T], L[:, 0:T - 1],
                                                           w1, cv[:, 1:T],
                                                           AL.mult, AL.add))
                            v.scalar_tensor_tensor(cv[:, 0:T], L[:, 0:T], w2,
                                                   cv[:, 0:T], AL.mult, AL.add
                                                   ).then_inc(s3.h, 1)
                    p.sec("vector", f_v)
                    s3.n += DIC

                    def f_s(s):
                        for dc in range(DIC):
                            s.wait_ge(s3.h, b3 + dc + 1)
                            s.activation(u_bf[:, dc * T:(dc + 1) * T],
                                         cvs[dc % 2][:, :], AF.Silu,
                                         bias=convb_c[:, dc:dc + 1]).then_inc(s4.h, 1)
                    p.sec("scalar", f_s)
                    s4.n += DIC

                with phase() as p:
                    b3, b4 = s3.n, s4.n

                    def f_t(t):
                        idx = 0
                        for wmat in (W_B_bf, W_C_bf):
                            for nh in range(NSC):
                                for th in range(2):
                                    if idx >= 2:
                                        t.wait_ge(s3.h, b3 + idx - 1)
                                    ins = None
                                    for c2 in range(2):
                                        for kc in range(DIC):
                                            ins = t.matmul(
                                                psu[idx % 2][:, c2 * 512:(c2 + 1) * 512],
                                                wmat[:, kc * NS + nh * 128:
                                                     kc * NS + (nh + 1) * 128],
                                                u_bf[:, kc * T + th * 1024 + c2 * 512:
                                                     kc * T + th * 1024 + (c2 + 1) * 512],
                                                start=(kc == 0), stop=(kc == DIC - 1))
                                    ins.then_inc(s4.h, 1)
                                    idx += 1
                    p.sec("tensor", f_t)
                    s4.n += 8

                    def f_v(v):
                        idx = 0
                        for dst in (BTt, CTt):
                            for nh in range(NSC):
                                for th in range(2):
                                    v.wait_ge(s4.h, b4 + idx + 1)
                                    v.tensor_copy(
                                        dst[:, nh * T + th * 1024:
                                            nh * T + (th + 1) * 1024],
                                        psu[idx % 2][:, :]).then_inc(s3.h, 1)
                                    idx += 1
                    p.sec("vector", f_v)
                    s3.n += 8
            for v_ in Lcs:
                af.release(v_)
            for v_ in cvs:
                af.release(v_)
            ab.release(W_B_bf)
            ab.release(W_C_bf)

            MILESTONES.append((pidx[0], f"p3_{tb}"))
            yscan = ab.alloc(DIC * T)
            dchunks = (af.alloc(T), af.alloc(T))
            uchunks = (ab.alloc(T), ab.alloc(T))
            dubc = (ab.alloc(T), ab.alloc(T))
            a_bufs = (af.alloc(NSC * T), af.alloc(NSC * T))
            x_t = ab.alloc(NSC * T)
            h_t = ab.alloc(NSC * T)
            g_t = ab.alloc(NSC * T)
            ystage = (ab.alloc(T, p=1),)
            tlns = (af.alloc(1024), af.alloc(1024))
            trl = af.alloc(1024)
            with contextlib.ExitStack() as sS:
                ps_y = psum(f"psy_{tb}", [1, T], F32, sS)

                def indcol(tensor_h, d):
                    base = tensor_h.ap()
                    return bass.AP(base.tensor, base.offset + d, [[128, 128], [0, 128]])

                for dc in range(DIC):
                    with contextlib.ExitStack() as sPrep:
                        ps_dp = (psum(f"psdp0_{tb}_{dc}", [128, 1024], F32, sPrep),
                                 psum(f"psdp1_{tb}_{dc}", [128, 1024], F32, sPrep))
                        with phase() as p:
                            def f_t(t, dc=dc, ps_dp=ps_dp):
                                for th in range(2):
                                    for c2 in range(2):
                                        for kc in range(DIC):
                                            t.matmul(ps_dp[th][:, c2 * 512:(c2 + 1) * 512],
                                                     W_delta_bf[:, kc * DI + dc * 128:
                                                                kc * DI + (dc + 1) * 128],
                                                     u_bf[:, kc * T + th * 1024 + c2 * 512:
                                                          kc * T + th * 1024 + (c2 + 1) * 512],
                                                     start=(kc == 0), stop=(kc == DIC - 1))
                            p.sec("tensor", f_t)
                        with phase() as p:
                            b4 = s4.n

                            def f_s(s, dc=dc, ps_dp=ps_dp):
                                for th in range(2):
                                    adep(s, s.activation(tlns[th][:, :], ps_dp[th][:, :],
                                                         AF.Abs,
                                                         bias=bdelta_c[:, dc:dc + 1]))
                                    adep(s, s.activation(tlns[th][:, :], tlns[th][:, :],
                                                         AF.Exp, scale=-1.0))
                                    s.activation(tlns[th][:, :], tlns[th][:, :],
                                                 AF.Ln, bias=1.0).then_inc(s4.h, 1)
                            p.sec("scalar", f_s)
                            s4.n += 2

                            def f_v(v, dc=dc, ps_dp=ps_dp):
                                dch = dchunks[dc % 2]
                                uch = uchunks[dc % 2]
                                for th in range(2):
                                    v.wait_ge(s4.h, b4 + th + 1)
                                    vdep(v, v.tensor_scalar(trl[:, :], ps_dp[th][:, :],
                                                            bdelta_c[:, dc:dc + 1], 0.0,
                                                            AL.add, AL.max))
                                    vdep(v, v.tensor_add(
                                        dch[:, th * 1024:(th + 1) * 1024],
                                        tlns[th][:, :], trl[:, :]))
                                    v.tensor_mul(uch[:, th * 1024:(th + 1) * 1024],
                                                 dch[:, th * 1024:(th + 1) * 1024],
                                                 u_bf[:, dc * T + th * 1024:
                                                      dc * T + (th + 1) * 1024])
                            p.sec("vector", f_v)

                    with contextlib.ExitStack() as sLoop:
                        ps_del = psum(f"psdel_{tb}_{dc}", [128, 1024], F32, sLoop)
                        ps_du = psum(f"psdu_{tb}_{dc}", [128, 1024], F32, sLoop)
                        with phase() as p:
                            bTb = s4.n
                            bAa = s2.n
                            bStg = s5.n
                            bAe = s6.n
                            bVx = s3.n
                            bVg = s7.n
                            bTy = s8.n

                            def f_t(t, dc=dc):
                                dch = dchunks[dc % 2]
                                uch = uchunks[dc % 2]

                                def ymm(j):
                                    t.wait_ge(s7.h, bVg + 2 * (j + 1))
                                    if j >= 1:
                                        t.wait_ge(s6.h, bAe + j)
                                    ins = None
                                    for c in range(4):
                                        for nh in range(NSC):
                                            ins = t.matmul(
                                                ps_y[0:1, c * 512:(c + 1) * 512],
                                                ones_bf[:, 0:1],
                                                g_t[:, nh * T + c * 512:
                                                    nh * T + (c + 1) * 512],
                                                start=(nh == 0), stop=(nh == NSC - 1))
                                    ins.then_inc(s8.h, 1)

                                for i in range(128):
                                    if i >= 2:
                                        ymm(i - 2)
                                    if i >= 1:
                                        t.wait_ge(s2.h, bAa + 6 * (i - 1) + 5)
                                    ins = None
                                    for c in range(2):
                                        ins = t.matmul(ps_del[:, c * 512:(c + 1) * 512],
                                                       indcol(ident, i),
                                                       dch[:, c * 512:(c + 1) * 512],
                                                       start=True, stop=True)
                                    ins.then_inc(s4.h, 1)
                                    if i >= 1:
                                        t.wait_ge(s2.h, bAa + 6 * (i - 1) + 6)
                                    ins = None
                                    for c in range(2):
                                        ins = t.matmul(ps_du[:, c * 512:(c + 1) * 512],
                                                       indcol(ident_bf, i),
                                                       uch[:, c * 512:(c + 1) * 512],
                                                       start=True, stop=True)
                                    ins.then_inc(s4.h, 1)
                                    t.wait_ge(s2.h, bAa + 6 * i + 2)
                                    ins = None
                                    for c in range(2):
                                        ins = t.matmul(ps_del[:, c * 512:(c + 1) * 512],
                                                       indcol(ident, i),
                                                       dch[:, 1024 + c * 512:
                                                           1024 + (c + 1) * 512],
                                                       start=True, stop=True)
                                    ins.then_inc(s4.h, 1)
                                    t.wait_ge(s2.h, bAa + 6 * i + 3)
                                    ins = None
                                    for c in range(2):
                                        ins = t.matmul(ps_du[:, c * 512:(c + 1) * 512],
                                                       indcol(ident_bf, i),
                                                       uch[:, 1024 + c * 512:
                                                           1024 + (c + 1) * 512],
                                                       start=True, stop=True)
                                    ins.then_inc(s4.h, 1)
                                ymm(126)
                                ymm(127)
                            p.sec("tensor", f_t)
                            s4.n += 512
                            s8.n += 128

                            def f_s(s, dc=dc):
                                def yev(j):
                                    s.wait_ge(s8.h, bTy + j + 1)
                                    if j >= 1:
                                        s.wait_ge(s5.h, bStg + 16 * j)
                                    s.activation(ystage[0][0:1, :], ps_y[0:1, :],
                                                 AF.Copy).then_inc(s6.h, 1)

                                for i in range(128):
                                    d = dc * 128 + i
                                    s.wait_ge(s4.h, bTb + 4 * i + 1)
                                    if i >= 2:
                                        s.wait_ge(s3.h, bVx + 6 * (i - 2) + 6)
                                    for nh in range(NSC):
                                        s.activation(
                                            a_bufs[i % 2][:, nh * T:nh * T + 1024],
                                            ps_del[:, :], AF.Exp,
                                            scale=AT[:, nh * DI + d:nh * DI + d + 1]
                                            ).then_inc(s2.h, 1)
                                    s.wait_ge(s4.h, bTb + 4 * i + 2)
                                    if i >= 2:
                                        s.wait_ge(s3.h, bVx + 6 * (i - 2) + 4)
                                    s.activation(dubc[i % 2][:, 0:1024], ps_du[:, :],
                                                 AF.Identity).then_inc(s2.h, 1)
                                    s.wait_ge(s4.h, bTb + 4 * i + 3)
                                    for nh in range(NSC):
                                        s.activation(
                                            a_bufs[i % 2][:, nh * T + 1024:
                                                          nh * T + 2048],
                                            ps_del[:, :], AF.Exp,
                                            scale=AT[:, nh * DI + d:nh * DI + d + 1]
                                            ).then_inc(s2.h, 1)
                                    s.wait_ge(s4.h, bTb + 4 * i + 4)
                                    s.activation(dubc[i % 2][:, 1024:2048],
                                                 ps_du[:, :], AF.Identity
                                                 ).then_inc(s2.h, 1)
                                    if i >= 2:
                                        yev(i - 2)
                                yev(126)
                                yev(127)
                            p.sec("scalar", f_s)
                            s2.n += 6 * 128
                            s6.n += 128

                            def f_v(v, dc=dc):
                                for i in range(128):
                                    for hh in range(2):
                                        v.wait_ge(s2.h, bAa + 6 * i + 3 * hh + 3)
                                        for nh in range(NSC):
                                            v.tensor_mul(
                                                x_t[:, nh * T + hh * 1024:
                                                    nh * T + (hh + 1) * 1024],
                                                BTt[:, nh * T + hh * 1024:
                                                    nh * T + (hh + 1) * 1024],
                                                dubc[i % 2][:, hh * 1024:
                                                            (hh + 1) * 1024]
                                                ).then_inc(s3.h, 1)
                                    v.wait_ge(s3.h, bVx + 6 * i + 4)
                                    v.wait_ge(s2.h, bAa + 6 * i + 5)
                                    for nh in range(NSC):
                                        v.tensor_tensor_scan(
                                            h_t[:, nh * T:(nh + 1) * T],
                                            a_bufs[i % 2][:, nh * T:(nh + 1) * T],
                                            x_t[:, nh * T:(nh + 1) * T],
                                            0.0, AL.mult, AL.add).then_inc(s3.h, 1)
                                    if i >= 1:
                                        v.wait_ge(s8.h, bTy + i)
                                    for nh in range(NSC):
                                        v.tensor_mul(
                                            g_t[:, nh * T:(nh + 1) * T],
                                            h_t[:, nh * T:(nh + 1) * T],
                                            CTt[:, nh * T:(nh + 1) * T]
                                            ).then_inc(s7.h, 1)
                            p.sec("vector", f_v)
                            s3.n += 6 * 128
                            s7.n += 256

                            def f_gp(gp, dc=dc):
                                for j in range(128):
                                    gp.wait_ge(s6.h, bAe + j + 1)
                                    gp.dma_start(
                                        out=yscan[j:j + 1, dc * T:(dc + 1) * T],
                                        in_=ystage[0][0:1, :]).then_inc(s5.h, 16)
                                gp.wait_ge(s5.h, bStg + 16 * 128)
                            p.sec("gpsimd", f_gp)
                            s5.n += 16 * 128
            for v_ in (dchunks + uchunks + dubc + a_bufs +
                       (x_t, h_t, g_t) + ystage + tlns + (trl,)):
                (v_.arena).release(v_)
            ab.release(BTt)
            ab.release(CTt)
            ab.release(W_delta_bf)

            MILESTONES.append((pidx[0], f"p4_{tb}"))
            gated = ab.alloc(DIC * T)
            with phase() as p:
                def f_v(v):
                    for dc in range(DIC):
                        vdep(v, v.scalar_tensor_tensor(
                            gated[:, dc * T:(dc + 1) * T],
                            u_bf[:, dc * T:(dc + 1) * T],
                            dvec_c[:, dc:dc + 1],
                            yscan[:, dc * T:(dc + 1) * T], AL.mult, AL.add))
                        v.tensor_mul(gated[:, dc * T:(dc + 1) * T],
                                     gated[:, dc * T:(dc + 1) * T],
                                     siluR[:, dc * T:(dc + 1) * T])
                p.sec("vector", f_v)
            ab.release(u_bf)
            ab.release(siluR)
            ab.release(yscan)

            x_mid = af.alloc(DMC * T)
            W_down_bf = ab.alloc(DIC * DM)
            with phase() as p:
                def f_gp(gp):
                    gp.dma_start(out=W_down_bf[:, :],
                                 in_=gi["W_down"].rearrange("(c p) o -> p c o",
                                                            p=128)
                                 ).then_inc(s2.h, 16)
                    gp.wait_ge(s2.h, s2.n + 16)
                p.sec("gpsimd", f_gp)
                s2.n += 16
            emit_tokenwise(b, "resid", x_mid, f"p4{tb}")
            dnsc = (af.alloc(1024), af.alloc(1024))
            with contextlib.ExitStack() as sB2:
                psd = (psum(f"psd0_{tb}", [128, 1024], F32, sB2),
                       psum(f"psd1_{tb}", [128, 1024], F32, sB2))
                with phase() as p:
                    b3, b4, b6 = s3.n, s4.n, s6.n

                    def f_t(t):
                        for do in range(DMC):
                            for th in range(2):
                                idx = do * 2 + th
                                if idx >= 2:
                                    t.wait_ge(s6.h, b6 + idx - 1)
                                ins = None
                                for c2 in range(2):
                                    for kc in range(DIC):
                                        ins = t.matmul(
                                            psd[idx % 2][:, c2 * 512:(c2 + 1) * 512],
                                            W_down_bf[:, kc * DM + do * 128:
                                                      kc * DM + (do + 1) * 128],
                                            gated[:, kc * T + th * 1024 + c2 * 512:
                                                  kc * T + th * 1024 + (c2 + 1) * 512],
                                            start=(kc == 0), stop=(kc == DIC - 1))
                                ins.then_inc(s4.h, 1)
                    p.sec("tensor", f_t)
                    s4.n += 16

                    def f_s(s):
                        for do in range(DMC):
                            for th in range(2):
                                idx = do * 2 + th
                                s.wait_ge(s4.h, b4 + idx + 1)
                                if idx >= 2:
                                    s.wait_ge(s3.h, b3 + idx - 1)
                                s.activation(dnsc[idx % 2][:, :],
                                             psd[idx % 2][:, :], AF.Identity,
                                             bias=bdown_c[:, do:do + 1]
                                             ).then_inc(s6.h, 1)
                    p.sec("scalar", f_s)
                    s6.n += 16

                    def f_v(v):
                        for do in range(DMC):
                            for th in range(2):
                                idx = do * 2 + th
                                v.wait_ge(s6.h, b6 + idx + 1)
                                sl = slice(do * T + th * 1024,
                                           do * T + (th + 1) * 1024)
                                v.tensor_add(x_mid[:, sl], x_mid[:, sl],
                                             dnsc[idx % 2][:, :]).then_inc(s3.h, 1)
                    p.sec("vector", f_v)
                    s3.n += 16
            af.release(dnsc[0])
            af.release(dnsc[1])
            ab.release(gated)
            ab.release(W_down_bf)

            MILESTONES.append((pidx[0], f"p5_{tb}"))
            tokT = ab.alloc(DMC * T)
            hT = ab.alloc(32 * 1024)
            w1bufs = (ab.alloc(8 * 256), ab.alloc(8 * 256))
            w2bufs = (ab.alloc(16 * 128), ab.alloc(16 * 128))
            mu3 = af.alloc(T)
            r13 = af.alloc(T)
            sq3 = af.alloc(T)
            with contextlib.ExitStack() as sLs:
                ps_st = psum(f"ps_st3_{tb}", [128, T], F32, sLs)
                with phase() as p:
                    def f_t(t):
                        for dc in range(DMC):
                            for c in range(4):
                                t.matmul(ps_st[:, c * 512:(c + 1) * 512],
                                         ones_f[:, :],
                                         x_mid[:, dc * T + c * 512:
                                               dc * T + (c + 1) * 512],
                                         start=(dc == 0), stop=(dc == DMC - 1))
                    p.sec("tensor", f_t)
                with phase() as p:
                    p.sec("vector", lambda v:
                          v.tensor_scalar_mul(mu3[:, :], ps_st[:, :], 1.0 / DM))
                with phase() as p:
                    b3, b4 = s3.n, s4.n

                    def f_s(s):
                        for dc in range(DMC):
                            if dc >= 1:
                                s.wait_ge(s4.h, b4 + dc)
                            s.activation(sq3[:, :], x_mid[:, dc * T:(dc + 1) * T],
                                         AF.Square).then_inc(s3.h, 1)
                    p.sec("scalar", f_s)
                    s3.n += DMC

                    def f_t(t):
                        for dc in range(DMC):
                            t.wait_ge(s3.h, b3 + dc + 1)
                            ins = None
                            for c in range(4):
                                ins = t.matmul(ps_st[:, c * 512:(c + 1) * 512],
                                               ones_f[:, :],
                                               sq3[:, c * 512:(c + 1) * 512],
                                               start=(dc == 0), stop=(dc == DMC - 1))
                            ins.then_inc(s4.h, 1)
                    p.sec("tensor", f_t)
                    s4.n += DMC
                with phase() as p:
                    def f_v(v):
                        vdep(v, v.tensor_scalar_mul(r13[:, :], ps_st[:, :], 1.0 / DM))
                        vdep(v, v.tensor_mul(sq3[:, :], mu3[:, :], mu3[:, :]))
                        vdep(v, v.tensor_sub(r13[:, :], r13[:, :], sq3[:, :]))
                        v.tensor_scalar_add(r13[:, :], r13[:, :], EPS)
                    p.sec("vector", f_v)
                with phase() as p:
                    def f_s(s):
                        adep(s, s.activation(sq3[:, :], r13[:, :], AF.Ln))
                        s.activation(r13[:, :], sq3[:, :], AF.Exp, scale=-0.5)
                    p.sec("scalar", f_s)
                with phase() as p:
                    def f_v(v):
                        for dc in range(DMC):
                            vdep(v, v.tensor_sub(sq3[:, :],
                                                 x_mid[:, dc * T:(dc + 1) * T],
                                                 mu3[:, :]))
                            v.tensor_mul(tokT[:, dc * T:(dc + 1) * T],
                                         sq3[:, :], r13[:, :])
                    p.sec("vector", f_v)
            af.release(r13)
            af.release(sq3)
            mask_tm = af.alloc(16 * 8)
            mask_fm = af.alloc(T, p=8)

            with contextlib.ExitStack() as sL2:
                ps_l = psum(f"ps_l_{tb}", [128, 8], F32, sL2)
                ps_mf = psum(f"ps_mf_{tb}", [8, T], F32, sL2)
                with phase() as p:
                    b3, b4 = s3.n, s4.n

                    def f_t(t):
                        for tc in range(16):
                            if tc >= 1:
                                t.wait_ge(s3.h, b3 + tc)
                            for dc in range(DMC):
                                t.matmul(ps_l[:, 0:8],
                                         x_mid[:, dc * T + tc * 128:
                                               dc * T + (tc + 1) * 128],
                                         gwT[:, dc * 8:(dc + 1) * 8],
                                         start=(dc == 0), stop=False)
                            t.matmul(ps_l[:, 0:8],
                                     mu3[0:1, tc * 128:(tc + 1) * 128],
                                     ngs_row[0:1, 0:8], start=False, stop=True
                                     ).then_inc(s4.h, 1)
                    p.sec("tensor", f_t)
                    s4.n += 16

                    def f_v(v):
                        for tc in range(16):
                            v.wait_ge(s4.h, b4 + tc + 1)
                            vdep(v, v.reduce_max(rmax[:, 0:1], ps_l[:, 0:8],
                                                 axis=AX.X))
                            v.tensor_single_scalar(mask_tm[:, tc * 8:(tc + 1) * 8],
                                                   ps_l[:, 0:8], rmax[:, 0:1],
                                                   AL.is_ge).then_inc(s3.h, 1)
                    p.sec("vector", f_v)
                    s3.n += 16
                with phase() as p:
                    def f_t(t):
                        for tc in range(16):
                            t.transpose(ps_mf[0:8, tc * 128:(tc + 1) * 128],
                                        mask_tm[:, tc * 8:(tc + 1) * 8],
                                        ident[:, :])
                    p.sec("tensor", f_t)
                with phase() as p:
                    p.sec("vector", lambda v:
                          v.tensor_copy(mask_fm[0:8, :], ps_mf[0:8, :]))
            af.release(mu3)
            mask_bc = af.alloc(T)
            tmpo = af.alloc(1024)

            MILESTONES.append((pidx[0], f"p6_{tb}"))
            with contextlib.ExitStack() as sF:
                ps_h = (psum(f"psh0_{tb}", [128, 1024], F32, sF),
                        psum(f"psh1_{tb}", [128, 1024], F32, sF))
                ps_o = (psum(f"pso0_{tb}", [128, 1024], F32, sF),
                        psum(f"pso1_{tb}", [128, 1024], F32, sF))

                for e in range(NEXP):
                    with phase() as p:
                        def f_sy(sy, e=e):
                            base = s1.n
                            sy.dma_start(out=mask_bc[0:1, :],
                                         in_=mask_fm[e:e + 1, :]).then_inc(s1.h, 16)
                            k, nd = 1, 1
                            while k < 128:
                                sy.wait_ge(s1.h, base + 16 * nd)
                                sy.dma_start(out=mask_bc[k:2 * k, 0:T],
                                             in_=mask_bc[0:k, 0:T]).then_inc(s1.h, 16)
                                k *= 2
                                nd += 1
                            sy.wait_ge(s1.h, base + 16 * 8)
                        p.sec("sync", f_sy)
                        s1.n += 16 * 8

                    with phase() as p:
                        bW1, bW2 = s1.n, s2.n
                        bTh, bAg = s5.n, s6.n
                        bTo, bVo = s7.n, s8.n

                        def f_gp(gp, e=e):
                            for ts in range(2):
                                for fq in range(16):
                                    j = ts * 16 + fq
                                    if j >= 2:
                                        gp.wait_ge(s5.h, bTh + 2 * (j - 1))
                                    gp.dma_start(
                                        out=w1bufs[j % 2][:, :],
                                        in_=gi["w1"][e, :, fq * 256:(fq + 1) * 256]
                                        .rearrange("(c p) f -> p c f", p=128)
                                        ).then_inc(s1.h, 16)
                                for dh in range(16):
                                    j = ts * 16 + dh
                                    doc, hf = divmod(dh, 2)
                                    if j >= 2:
                                        gp.wait_ge(s7.h, bTo + (j - 1))
                                    gp.dma_start(
                                        out=w2bufs[j % 2][:, :],
                                        in_=gi["w2"][e, hf * 2048:(hf + 1) * 2048,
                                                     doc * 128:(doc + 1) * 128]
                                        .rearrange("(c p) o -> p c o", p=128)
                                        ).then_inc(s2.h, 16)
                        p.sec("gpsimd", f_gp)
                        s1.n += 16 * 32
                        s2.n += 16 * 32

                        def f_t(t, e=e):
                            for ts in range(2):
                                for fq in range(16):
                                    j = ts * 16 + fq
                                    t.wait_ge(s1.h, bW1 + 16 * (j + 1))
                                    for sub in range(2):
                                        fc = fq * 2 + sub
                                        jj = j * 2 + sub
                                        if jj >= 2:
                                            t.wait_ge(s6.h, bAg + (jj - 1))
                                        ins = None
                                        for c2 in range(2):
                                            for kc in range(DMC):
                                                ins = t.matmul(
                                                    ps_h[fc % 2][:, c2 * 512:(c2 + 1) * 512],
                                                    w1bufs[j % 2][:, kc * 256 + sub * 128:
                                                                  kc * 256 + (sub + 1) * 128],
                                                    tokT[:, kc * T + ts * 1024 + c2 * 512:
                                                         kc * T + ts * 1024 + (c2 + 1) * 512],
                                                    start=(kc == 0), stop=(kc == DMC - 1))
                                        ins.then_inc(s5.h, 1)
                                for dh in range(16):
                                    j = ts * 16 + dh
                                    doc, hf = divmod(dh, 2)
                                    t.wait_ge(s2.h, bW2 + 16 * (j + 1))
                                    t.wait_ge(s6.h, bAg + 32 * (ts + 1))
                                    if hf == 0 and ts * 8 + doc >= 2:
                                        t.wait_ge(s8.h, bVo + (ts * 8 + doc - 1))
                                    ins = None
                                    for c2 in range(2):
                                        for fc in range(16):
                                            gfc = hf * 16 + fc
                                            ins = t.matmul(
                                                ps_o[doc % 2][:, c2 * 512:(c2 + 1) * 512],
                                                w2bufs[j % 2][:, fc * 128:(fc + 1) * 128],
                                                hT[:, gfc * 1024 + c2 * 512:
                                                    gfc * 1024 + (c2 + 1) * 512],
                                                start=(gfc == 0), stop=(gfc == 31))
                                    ins.then_inc(s7.h, 1)
                        p.sec("tensor", f_t)
                        s5.n += 64
                        s7.n += 32

                        def f_s(s, e=e):
                            for ts in range(2):
                                for fc in range(32):
                                    j = ts * 32 + fc
                                    s.wait_ge(s5.h, bTh + j + 1)
                                    s.activation(hT[:, fc * 1024:(fc + 1) * 1024],
                                                 ps_h[fc % 2][:, :], AF.Gelu
                                                 ).then_inc(s6.h, 1)
                        p.sec("scalar", f_s)
                        s6.n += 64

                        def f_v(v, e=e):
                            for ts in range(2):
                                for doc in range(8):
                                    v.wait_ge(s7.h, bTo + ts * 16 + doc * 2 + 2)
                                    vdep(v, v.tensor_mul(
                                        tmpo[:, :], ps_o[doc % 2][:, :],
                                        mask_bc[:, ts * 1024:(ts + 1) * 1024]))
                                    v.tensor_add(
                                        x_mid[:, doc * T + ts * 1024:
                                              doc * T + (ts + 1) * 1024],
                                        x_mid[:, doc * T + ts * 1024:
                                              doc * T + (ts + 1) * 1024],
                                        tmpo[:, :]).then_inc(s8.h, 1)
                        p.sec("vector", f_v)
                        s8.n += 16
            for v_ in (tokT, hT, w1bufs[0], w1bufs[1], w2bufs[0], w2bufs[1]):
                ab.release(v_)
            for v_ in (mask_tm, mask_bc, mask_fm, tmpo):
                af.release(v_)

            MILESTONES.append((pidx[0], f"p7_{tb}"))
            otm = (af.alloc(DM), af.alloc(DM))
            with contextlib.ExitStack() as sO:
                ps_ot = psum(f"ps_ot_{tb}", [128, DM], F32, sO)
                with phase() as p:
                    b3, b4 = s3.n, s4.n
                    bSt = sOut.n

                    def f_t(t, b=b):
                        for i in range(16):
                            if i >= 1:
                                t.wait_ge(s3.h, b3 + i)
                            ins = None
                            for dc in range(DMC):
                                ins = t.transpose(ps_ot[:, dc * 128:(dc + 1) * 128],
                                                  x_mid[:, dc * T + i * 128:
                                                        dc * T + (i + 1) * 128],
                                                  ident[:, :])
                            ins.then_inc(s4.h, 1)
                    p.sec("tensor", f_t)
                    s4.n += 16

                    def f_v(v, b=b):
                        for i in range(16):
                            v.wait_ge(s4.h, b4 + i + 1)
                            if i >= 2:
                                v.wait_ge(sOut.h, bSt + 16 * (i - 1))
                            v.tensor_copy(otm[i % 2][:, :], ps_ot[:, :]
                                          ).then_inc(s3.h, 1)
                    p.sec("vector", f_v)
                    s3.n += 16

                    def f_sy(sy, b=b):
                        for i in range(16):
                            sy.wait_ge(s3.h, b3 + i + 1)
                            sy.dma_start(out=y_out[b, i * 128:(i + 1) * 128, :],
                                         in_=otm[i % 2][:, :]).then_inc(sOut.h, 16)
                        sy.wait_ge(sOut.h, bSt + 16 * 16)
                    p.sec("sync", f_sy)
                    sOut.n += 16 * 16
            af.release(otm[0])
            af.release(otm[1])
            af.release(x_mid)

        with phase() as p:
            p.sec("sync", lambda sy: sy.wait_ge(sOut.h, sOut.n))
    return nc


def kernel(**inputs):
    x = np.ascontiguousarray(inputs["L0_attn_x"])
    nm = {
        "ln_gamma": "L0_attn_ln_gamma", "ln_beta": "L0_attn_ln_beta",
        "W_up": "L0_attn_W_up", "b_up": "L0_attn_b_up",
        "conv_w": "L0_attn_conv_w", "conv_b": "L0_attn_conv_b",
        "W_delta": "L0_attn_W_delta", "b_delta": "L0_attn_b_delta",
        "W_B": "L0_attn_W_B", "W_C": "L0_attn_W_C",
        "A_log": "L0_attn_A_log", "Dvec": "L0_attn_D",
        "W_down": "L0_attn_W_down", "b_down": "L0_attn_b_down",
        "gate_w": "L0_ffn_gate_w", "w1": "L0_ffn_w1", "w2": "L0_ffn_w2",
    }
    weights = {k: np.ascontiguousarray(inputs[v]) for k, v in nm.items()}
    nc = build()
    in_maps = []
    for c in range(8):
        m = dict(weights)
        m["x_in"] = x[2 * c:2 * c + 2]
        in_maps.append(m)
    res = run_bass_kernel_spmd(nc, in_maps, core_ids=list(range(8)))
    return np.concatenate([r["y_out"] for r in res.results], axis=0)
